# version 30
# baseline (speedup 1.0000x reference)
import os
import numpy as np
import ml_dtypes
from contextlib import ExitStack

KDBG = bool(int(os.environ.get("KDBG", "0")))
KDBG2 = bool(int(os.environ.get("KDBG2", "0")))
SELFSYNC = os.environ.get("KSELF", "vector,scalar,gpsimd")

import concourse.bass as bass
import concourse.mybir as mybir
from concourse.bass_utils import run_bass_kernel_spmd

F32 = mybir.dt.float32
BF16 = mybir.dt.bfloat16
AF = mybir.ActivationFunctionType
ALU = mybir.AluOpType
BD = ml_dtypes.bfloat16

DIM = 2048
H = 16
DH = 64
NCH = 16
TPC = 1024
NCORES = 8
IFFN = 8192
LAYER_NUM = 12
LAM_INIT = 0.8 - 0.6 * float(np.exp(-0.3 * LAYER_NUM))
SCALE = DH ** -0.5
EPS = 1e-5
RMS_BIAS = float(np.log(1.0 - LAM_INIT))

CONSTS_F32 = [0.125, -1.0, -0.5, 1e-5, 1.0 / 128.0, RMS_BIAS, 1.0 / DIM, float(LAM_INIT), 128.0 * EPS,
              1.5, 0.5, 1.0, 2.0]
CONSTS_B16 = [-1.0, 1.0, 1.5, 0.5, -0.5, 0.125, 2.0, 0.0]

ENGS = ("sync", "gpsimd", "tensor", "scalar", "vector")


class Sem:
    def __init__(self, stk, nc, name):
        self.h = stk.enter_context(nc.semaphore(name))
        self.n = 0
        self.name = name


class Prog:

    def __init__(self):
        self.streams = {e: [] for e in ENGS}
        self.waited = {}
        self.chain = {}

    def op(self, eng, build, waits=(), inc=None, inck=1):
        ch = self.chain.get(eng)
        if ch is not None:
            last = getattr(self, "_last_done", {}).get(eng)
            if last is not None:
                waits = list(waits) + [last]
        wl = []
        for s, v in waits:
            key = (eng, s.name)
            if v > self.waited.get(key, -1):
                wl.append((s.h, v))
                self.waited[key] = v
        ih = None
        if inc is not None:
            ih = (inc.h, inck)
            inc.n += inck
        ch2 = None
        if ch is not None:
            if not hasattr(self, "_last_done"):
                self._last_done = {}
            if inc is not None:
                self._last_done[eng] = (inc, inc.n)
            else:
                k = 16 if inck == 16 else 1
                ch.n += k
                ch2 = (ch.h, k)
                self._last_done[eng] = (ch, ch.n)
        def thunk(e, wl=tuple(wl), build=build, ih=ih, ch2=ch2):
            for h, v in wl:
                e.wait_ge(h, v)
            ins = build(e)
            if ins is not None:
                if ih is not None:
                    ins.then_inc(ih[0], ih[1])
                if ch2 is not None:
                    ins.then_inc(ch2[0], ch2[1])
        self.streams[eng].append(thunk)
        return inc.n if inc is not None else None

    def flush(self, nc, name):
        streams = self.streams
        self.streams = {e: [] for e in ENGS}
        with nc.Block(name) as blk:
            if streams["sync"]:
                blk.sync(lambda e, s=streams["sync"]: [f(e) for f in s])
            if streams["gpsimd"]:
                blk.gpsimd(lambda e, s=streams["gpsimd"]: [f(e) for f in s])
            if streams["tensor"]:
                blk.tensor(lambda e, s=streams["tensor"]: [f(e) for f in s])
            if streams["scalar"]:
                blk.scalar(lambda e, s=streams["scalar"]: [f(e) for f in s])
            if streams["vector"]:
                blk.vector(lambda e, s=streams["vector"]: [f(e) for f in s])


def build_program():
    nc = bass.Bass("TRN2", target_bir_lowering=False)
    stk = ExitStack()

    xT = nc.dram_tensor("xT", [DIM, TPC], F32, kind="ExternalInput").ap()
    wqt = nc.dram_tensor("wqt", [DIM, DIM], BF16, kind="ExternalInput").ap()
    wkt = nc.dram_tensor("wkt", [DIM, DIM], BF16, kind="ExternalInput").ap()
    wvt = nc.dram_tensor("wvt", [DIM, DIM], BF16, kind="ExternalInput").ap()
    wot = nc.dram_tensor("wot", [DIM, DIM], BF16, kind="ExternalInput").ap()
    wgp = nc.dram_tensor("wgp", [DIM, IFFN], BF16, kind="ExternalInput").ap()
    wup = nc.dram_tensor("wup", [DIM, IFFN], BF16, kind="ExternalInput").ap()
    wdp = nc.dram_tensor("wdp", [IFFN, DIM], BF16, kind="ExternalInput").ap()
    bo_d = nc.dram_tensor("bo", [1, DIM], BF16, kind="ExternalInput").ap()
    lqk_d = nc.dram_tensor("lqk", [1, 256], F32, kind="ExternalInput").ap()
    b2a_d = nc.dram_tensor("b2a", [32, 2], BF16, kind="ExternalInput").ap()
    sel2b_d = nc.dram_tensor("sel2b", [2, 32], BF16, kind="ExternalInput").ap()
    sel2f_d = nc.dram_tensor("sel2f", [2, 32], F32, kind="ExternalInput").ap()
    o128b_d = nc.dram_tensor("o128b", [128, 1], BF16, kind="ExternalInput").ap()
    o128f_d = nc.dram_tensor("o128f", [1, 128], F32, kind="ExternalInput").ap()
    o128br_d = nc.dram_tensor("o128br", [1, 128], BF16, kind="ExternalInput").ap()
    o2f_d = nc.dram_tensor("o2f", [1, 2], F32, kind="ExternalInput").ap()
    oNb_d = nc.dram_tensor("oNb", [1, TPC], BF16, kind="ExternalInput").ap()
    cvf_d = nc.dram_tensor("cvf", [1, 128], F32, kind="ExternalInput").ap()
    cvb_d = nc.dram_tensor("cvb", [1, 256], BF16, kind="ExternalInput").ap()
    y_d = nc.dram_tensor("y", [DIM, TPC], F32, kind="ExternalOutput").ap()
    Vd = nc.dram_tensor("Vd", [TPC, DIM], BF16, kind="Internal").ap()
    Dd = nc.dram_tensor("Dd", [TPC, 256], BF16, kind="Internal").ap()
    if KDBG2:
        dbg_outn2 = nc.dram_tensor("dbg_outn2", [128, NCH * TPC], BF16, kind="ExternalOutput").ap()
        dbg_vd2 = nc.dram_tensor("dbg_vd2", [TPC, DIM], BF16, kind="ExternalOutput").ap()
        dbg_lam2 = nc.dram_tensor("dbg_lam2", [1, 1], F32, kind="ExternalOutput").ap()
        dbg_sel2 = nc.dram_tensor("dbg_sel2", [2, 32], BF16, kind="ExternalOutput").ap()
        dbg_dd2 = nc.dram_tensor("dbg_dd2", [TPC, 256], BF16, kind="ExternalOutput").ap()
        dbg_rbd2 = nc.dram_tensor("dbg_rbd2", [128, 1024], BF16, kind="ExternalOutput").ap()
        dbg_v82 = nc.dram_tensor("dbg_v82", [128, 1024], BF16, kind="ExternalOutput").ap()
        dbg_osb2 = nc.dram_tensor("dbg_osb2", [128, 1024], F32, kind="ExternalOutput").ap()
        dbg_rst2 = nc.dram_tensor("dbg_rst2", [1, 1024], BF16, kind="ExternalOutput").ap()
        dbg_es2 = nc.dram_tensor("dbg_es2", [32, 2048], BF16, kind="ExternalOutput").ap()
        dbg_rb2x = nc.dram_tensor("dbg_rb2x", [128, 2048], BF16, kind="ExternalOutput").ap()
        dbg_lk2x = nc.dram_tensor("dbg_lk2x", [128, 2048], BF16, kind="ExternalOutput").ap()
        dbg_zb2x = nc.dram_tensor("dbg_zb2x", [32, 2048], BF16, kind="ExternalOutput").ap()
        dbg_kt2 = nc.dram_tensor("dbg_kt2", [128, NCH * TPC], BF16, kind="ExternalOutput").ap()
        dbg_qt2 = nc.dram_tensor("dbg_qt2", [128, NCH * TPC], BF16, kind="ExternalOutput").ap()
    if KDBG:
        dbg_ln1 = nc.dram_tensor("dbg_ln1", [128, NCH * TPC], BF16, kind="ExternalOutput").ap()
        dbg_qt = nc.dram_tensor("dbg_qt", [128, NCH * TPC], BF16, kind="ExternalOutput").ap()
        dbg_kt = nc.dram_tensor("dbg_kt", [128, NCH * TPC], BF16, kind="ExternalOutput").ap()
        dbg_vd = nc.dram_tensor("dbg_vd", [TPC, DIM], BF16, kind="ExternalOutput").ap()
        dbg_outn = nc.dram_tensor("dbg_outn", [128, NCH * TPC], BF16, kind="ExternalOutput").ap()
        dbg_x2 = nc.dram_tensor("dbg_x2", [128, NCH * TPC], F32, kind="ExternalOutput").ap()
        dbg_e0 = nc.dram_tensor("dbg_e0", [32, 1024], BF16, kind="ExternalOutput").ap()
        dbg_e1 = nc.dram_tensor("dbg_e1", [32, 1024], BF16, kind="ExternalOutput").ap()
        dbg_zr = nc.dram_tensor("dbg_zr", [2, 1024], BF16, kind="ExternalOutput").ap()
        dbg_zb0 = nc.dram_tensor("dbg_zb0", [32, 1024], BF16, kind="ExternalOutput").ap()
        dbg_zb1 = nc.dram_tensor("dbg_zb1", [32, 1024], BF16, kind="ExternalOutput").ap()
        dbg_dm = nc.dram_tensor("dbg_dm", [32, 512], BF16, kind="ExternalOutput").ap()
        dbg_rbd = nc.dram_tensor("dbg_rbd", [128, 1024], BF16, kind="ExternalOutput").ap()
        dbg_v8 = nc.dram_tensor("dbg_v8", [128, 1024], BF16, kind="ExternalOutput").ap()
        dbg_osb = nc.dram_tensor("dbg_osb", [128, 1024], F32, kind="ExternalOutput").ap()
        dbg_rst = nc.dram_tensor("dbg_rst", [1, 1024], BF16, kind="ExternalOutput").ap()
        dbg_lnms = nc.dram_tensor("dbg_lnms", [1, 1024], F32, kind="ExternalOutput").ap()
        dbg_x0 = nc.dram_tensor("dbg_x0", [128, NCH * TPC], F32, kind="ExternalOutput").ap()
        dbg_rb = nc.dram_tensor("dbg_rb", [128, TPC], BF16, kind="ExternalOutput").ap()
        dbg_bmu = nc.dram_tensor("dbg_bmu", [128, TPC], F32, kind="ExternalOutput").ap()
        dbg_msq = nc.dram_tensor("dbg_msq", [1, TPC], F32, kind="ExternalOutput").ap()
        dbg_var = nc.dram_tensor("dbg_var", [1, TPC], F32, kind="ExternalOutput").ap()
        dbg_lnv = nc.dram_tensor("dbg_lnv", [1, TPC], F32, kind="ExternalOutput").ap()
        dbg_rsd = nc.dram_tensor("dbg_rsd", [1, TPC], F32, kind="ExternalOutput").ap()
        dbg_tb = nc.dram_tensor("dbg_tb", [128, NCH * TPC], BF16, kind="ExternalOutput").ap()
        dbg_ms = nc.dram_tensor("dbg_ms", [1, 1024], F32, kind="ExternalOutput").ap()
        dbg_sq = nc.dram_tensor("dbg_sq", [128, 1024], BF16, kind="ExternalOutput").ap()
        dbg_lk = nc.dram_tensor("dbg_lk", [128, 2048], BF16, kind="ExternalOutput").ap()
        dbg_sel = nc.dram_tensor("dbg_sel", [2, 32], BF16, kind="ExternalOutput").ap()
        dbg_lam = nc.dram_tensor("dbg_lam", [1, 4], F32, kind="ExternalOutput").ap()
        dbg_cvf = nc.dram_tensor("dbg_cvf", [1, len(CONSTS_F32)], F32, kind="ExternalOutput").ap()
        dbg_r = nc.dram_tensor("dbg_r", [128, 1024], BF16, kind="ExternalOutput").ap()
        dbg_ln2 = nc.dram_tensor("dbg_ln2", [128, NCH * TPC], BF16, kind="ExternalOutput").ap()

    def dbg_dump(name, dst, srcf, waits=()):
        if not KDBG:
            return
        P.op("gpsimd", lambda e: e.dma_start(dst, srcf()), waits=list(waits), inc=sG, inck=16)
        g_w = sG.n
        P.op("gpsimd", lambda e: e.nop(), waits=[(sG, g_w)])
        P.flush(nc, "dbg_" + name)

    x_sb = stk.enter_context(nc.sbuf_tensor("x_sb", [128, NCH * TPC], F32))
    AB = stk.enter_context(nc.sbuf_tensor("AB", [128, NCH * TPC], BF16))
    QT = stk.enter_context(nc.sbuf_tensor("QT", [128, NCH * TPC], BF16))
    KT = stk.enter_context(nc.sbuf_tensor("KT", [128, NCH * TPC], BF16))
    bo_sb = stk.enter_context(nc.sbuf_tensor("bo_sb", [1, DIM], BF16))
    lqk_sb = stk.enter_context(nc.sbuf_tensor("lqk_sb", [1, 256], F32))
    b2a = stk.enter_context(nc.sbuf_tensor("b2a_s", [32, 2], BF16))
    sel2b = stk.enter_context(nc.sbuf_tensor("sel2b_s", [2, 32], BF16))
    sel2f = stk.enter_context(nc.sbuf_tensor("sel2f_s", [2, 32], F32))
    sel2l = stk.enter_context(nc.sbuf_tensor("sel2l_s", [2, 32], BF16))
    o128b = stk.enter_context(nc.sbuf_tensor("o128b_s", [128, 1], BF16))
    o128f = stk.enter_context(nc.sbuf_tensor("o128f_s", [1, 128], F32))
    o128br = stk.enter_context(nc.sbuf_tensor("o128br_s", [1, 128], BF16))
    o2f = stk.enter_context(nc.sbuf_tensor("o2f_s", [1, 2], F32))
    oNb = stk.enter_context(nc.sbuf_tensor("oNb_s", [1, TPC], BF16))
    lam_t = stk.enter_context(nc.sbuf_tensor("lam_t", [1, 64], F32))
    lam12 = stk.enter_context(nc.sbuf_tensor("lam12", [1, 2], F32))
    lam_e = stk.enter_context(nc.sbuf_tensor("lam_e", [1, 2], F32))
    lam_s = stk.enter_context(nc.sbuf_tensor("lam_s", [1, 1], F32))
    lam2 = stk.enter_context(nc.sbuf_tensor("lam2", [2, 1], F32))
    lamdiag = stk.enter_context(nc.sbuf_tensor("lamdiag", [2, 2], BF16))
    cvf = stk.enter_context(nc.sbuf_tensor("cvf_s", [1, 128], F32))
    cvb = stk.enter_context(nc.sbuf_tensor("cvb_s", [1, 256], BF16))
    for i, v in enumerate(CONSTS_F32):
        nc.const_aps.register(F32, v, cvf[0:1, i:i + 1])
    for i, v in enumerate(CONSTS_B16):
        nc.const_aps.register(BF16, v, cvb[0:1, i:i + 1])

    sW = Sem(stk, nc, "sW")
    sW2 = Sem(stk, nc, "sW2")
    sG2 = Sem(stk, nc, "sG2")
    sG3 = Sem(stk, nc, "sG3")
    sS2 = Sem(stk, nc, "sS2")
    sWd = Sem(stk, nc, "sWd")
    sWd2 = Sem(stk, nc, "sWd2")
    sWg = Sem(stk, nc, "sWg")
    sGc = Sem(stk, nc, "sGc")
    sG = Sem(stk, nc, "sG")
    sS = Sem(stk, nc, "sS")
    sT = Sem(stk, nc, "sT")
    sV = Sem(stk, nc, "sV")
    sC = Sem(stk, nc, "sC")

    P = Prog()
    for _e in [s for s in SELFSYNC.split(",") if s]:
        P.chain[_e] = Sem(stk, nc, "ch_" + _e)

    for dst, src in ((bo_sb, bo_d), (lqk_sb, lqk_d), (b2a, b2a_d), (sel2b, sel2b_d),
                     (sel2f, sel2f_d), (o128b, o128b_d), (o128f, o128f_d),
                     (o128br, o128br_d), (o2f, o2f_d), (oNb, oNb_d),
                     (cvf, cvf_d), (cvb, cvb_d)):
        P.op("sync", lambda e, d=dst, s=src: e.dma_start(d[:, :], s), inc=sW, inck=16)
    w_consts = sW.n
    xv = xT.rearrange("(c p) t -> p c t", p=128)
    for i in range(4):
        P.op("gpsimd", lambda e, i=i: e.dma_start(
            x_sb[:, :].rearrange("p (c t) -> p c t", c=NCH)[:, 4 * i:4 * i + 4, :],
            xv[:, 4 * i:4 * i + 4, :]), inc=sG, inck=16)
    w_x = sG.n

    P.op("vector", lambda e: e.tensor_tensor(lam_t[:, :], lqk_sb[0:1, 0:64], lqk_sb[0:1, 128:192], ALU.mult),
         waits=[(sW, w_consts)], inc=sV)
    P.op("vector", lambda e: e.reduce_sum(lam12[0:1, 0:1], lam_t[:, :], axis=mybir.AxisListType.X), inc=sV)
    P.op("vector", lambda e: e.tensor_tensor(lam_t[:, :], lqk_sb[0:1, 64:128], lqk_sb[0:1, 192:256], ALU.mult), inc=sV)
    P.op("vector", lambda e: e.reduce_sum(lam12[0:1, 1:2], lam_t[:, :], axis=mybir.AxisListType.X), inc=sV)
    v_l = sV.n
    P.op("scalar", lambda e: e.activation(lam_e[:, :], lam12[:, :], AF.Exp), waits=[(sV, v_l)], inc=sC)
    c_l = sC.n
    c_lam_exp = sC.n
    P.op("sync", lambda e: e.nop(), waits=[(sW, sW.n)])
    P.op("gpsimd", lambda e: e.nop(), waits=[(sG, sG.n)])
    P.flush(nc, "init")
    if True:

        def emit_ln2(tag):
            with nc.psum_tensor("pSx" + tag, [1, TPC], F32) as pSx, \
                 nc.psum_tensor("pSxx" + tag, [1, TPC], F32) as pSxx, \
                 nc.psum_tensor("pBmu" + tag, [128, TPC], F32) as pBmu, \
                 nc.psum_tensor("pBr" + tag, [128, TPC], F32) as pBr, \
                 nc.sbuf_tensor("xb" + tag, [128, 2 * TPC], BF16) as xb, \
                 nc.sbuf_tensor("sqb" + tag, [128, 2 * TPC], BF16) as sqb, \
                 nc.sbuf_tensor("mu" + tag, [1, TPC], F32) as mu_sb, \
                 nc.sbuf_tensor("msq" + tag, [1, TPC], F32) as msq_sb, \
                 nc.sbuf_tensor("rb" + tag, [128, TPC], BF16) as rb_bf, \
                 nc.sbuf_tensor("m2" + tag, [1, TPC], F32) as m2_sb, \
                 nc.sbuf_tensor("tdbg" + tag, [128, 8 * TPC if KDBG else 2 * TPC], BF16) as tdbg:
                vdone = [0, 0]
                for c in range(NCH):
                    par = c % 2
                    xc = x_sb[:, TPC * c:TPC * (c + 1)]
                    xbp = xb[:, TPC * par:TPC * (par + 1)]
                    sqp = sqb[:, TPC * par:TPC * (par + 1)]
                    waits = [(sG, w_x)]
                    if c >= 2:
                        waits.append((sT, vdone[par]))
                    P.op("vector", lambda e, xbp=xbp, xc=xc: e.tensor_copy(xbp, xc), waits=waits, inc=sV)
                    P.op("vector", lambda e, sqp=sqp, xbp=xbp: e.tensor_tensor(sqp, xbp, xbp, ALU.mult), inc=sV)
                    v_c = sV.n
                    for hseg in range(2):
                        P.op("tensor", lambda e, xbp=xbp, hseg=hseg, c=c: e.matmul(
                            pSx[:, 512 * hseg:512 * (hseg + 1)], o128b[:, :],
                            xbp[:, 512 * hseg:512 * (hseg + 1)],
                            start=(c == 0), stop=(c == NCH - 1), skip_group_check=True),
                            waits=[(sV, v_c)], inc=sT)
                        P.op("tensor", lambda e, sqp=sqp, hseg=hseg, c=c: e.matmul(
                            pSxx[:, 512 * hseg:512 * (hseg + 1)], o128b[:, :],
                            sqp[:, 512 * hseg:512 * (hseg + 1)],
                            start=(c == 0), stop=(c == NCH - 1), skip_group_check=True), inc=sT)
                    vdone[par] = sT.n
                t_sums = sT.n
                P.op("vector", lambda e: e.tensor_single_scalar(mu_sb[:, :], pSx[:, :], 1.0 / DIM, ALU.mult),
                     waits=[(sT, t_sums)], inc=sV)
                P.op("vector", lambda e: e.tensor_single_scalar(msq_sb[:, :], pSxx[:, :], 1.0 / DIM, ALU.mult), inc=sV)
                v_mu = sV.n
                if KDBG and tag == "1":
                    P.op("gpsimd", lambda e: e.dma_start(dbg_msq, msq_sb[:, :]), waits=[(sV, v_mu)], inc=sG, inck=16)
                for hseg in range(2):
                    P.op("tensor", lambda e, hseg=hseg: e.matmul(
                        pBmu[:, 512 * hseg:512 * (hseg + 1)], o128f[:, :],
                        mu_sb[0:1, 512 * hseg:512 * (hseg + 1)], start=True, stop=True),
                        waits=[(sV, v_mu)], inc=sT)
                t_bmu = sT.n
                P.op("vector", lambda e: e.tensor_tensor(m2_sb[:, :], mu_sb[:, :], mu_sb[:, :], ALU.mult),
                     waits=[(sT, t_bmu)], inc=sV)
                P.op("vector", lambda e: e.tensor_tensor(msq_sb[:, :], msq_sb[:, :], m2_sb[:, :], ALU.subtract), inc=sV)
                P.op("vector", lambda e: e.tensor_single_scalar(msq_sb[:, :], msq_sb[:, :], EPS, ALU.add), inc=sV)
                v_var = sV.n
                if KDBG and tag == "1":
                    P.op("gpsimd", lambda e: e.dma_start(dbg_var, msq_sb[:, :]), waits=[(sV, v_var)], inc=sG, inck=16)
                    gv_ = sG.n
                P.op("scalar", lambda e: e.activation(mu_sb[:, :], msq_sb[:, :], AF.Ln),
                     waits=[(sV, v_var)] + ([(sG, gv_)] if (KDBG and tag == "1") else []), inc=sC)
                c_ln = sC.n
                if KDBG and tag == "1":
                    P.op("gpsimd", lambda e: e.dma_start(dbg_lnv, mu_sb[:, :]), waits=[(sC, c_ln)], inc=sG, inck=16)
                P.op("scalar", lambda e: e.activation(msq_sb[:, :], mu_sb[:, :], AF.Exp, scale=-0.5),
                     waits=[(sC, c_ln)] + ([(sG, sG.n)] if (KDBG and tag == "1") else []), inc=sC)
                c_r = sC.n
                if KDBG and tag == "1":
                    P.op("gpsimd", lambda e: e.dma_start(dbg_rsd, msq_sb[:, :]), waits=[(sC, c_r)], inc=sG, inck=16)
                for hseg in range(2):
                    P.op("tensor", lambda e, hseg=hseg: e.matmul(
                        pBr[:, 512 * hseg:512 * (hseg + 1)], o128f[:, :],
                        msq_sb[0:1, 512 * hseg:512 * (hseg + 1)], start=True, stop=True),
                        waits=[(sC, c_r)], inc=sT)
                t_br = sT.n
                P.op("vector", lambda e: e.tensor_copy(rb_bf[:, :], pBr[:, :]),
                     waits=[(sT, t_br)], inc=sV)
                c_rb = sC.n
                v_rb = sV.n
                if KDBG and tag == "1":
                    P.op("gpsimd", lambda e: e.dma_start(dbg_rb, rb_bf[:, :]), waits=[(sV, v_rb)], inc=sG, inck=16)
                    P.op("vector", lambda e: e.tensor_copy(xb[:, 0:TPC], pBmu[:, :]), waits=[(sV, v_rb)], inc=sV)
                    vdbg = sV.n
                    P.op("gpsimd", lambda e: e.dma_start(dbg_bmu, xb[:, 0:TPC]), waits=[(sV, vdbg)], inc=sG, inck=16)
                    gdbg = sG.n
                for c in range(NCH):
                    par = c % 2
                    xc = x_sb[:, TPC * c:TPC * (c + 1)]
                    tb = tdbg[:, TPC * (c % 8):TPC * (c % 8 + 1)] if KDBG else tdbg[:, TPC * par:TPC * (par + 1)]
                    abc = AB[:, TPC * c:TPC * (c + 1)]
                    waits_ap = [(sV, v_rb)]
                    if KDBG and tag == "1" and c == 0:
                        waits_ap.append((sG, gdbg))
                    P.op("vector", lambda e, tb=tb, xc=xc: e.tensor_tensor(tb, xc, pBmu[:, :], ALU.subtract),
                         waits=waits_ap, inc=sV)
                    P.op("vector", lambda e, abc=abc, tb=tb: e.tensor_tensor(abc, tb, rb_bf[:, :], ALU.mult),
                         inc=sV)
                v_apl = sV.n
                if KDBG and tag == "1":
                    P.op("gpsimd", lambda e: e.dma_start(dbg_tb[:, 0:8 * TPC], tdbg[:, :]), waits=[(sV, v_apl)], inc=sG, inck=16)
                P.flush(nc, "ln" + tag)

        emit_ln2("1")
        if KDBG:
            dbg_dump("ln1", dbg_ln1, lambda: AB[:, :])
            dbg_dump("x0", dbg_x0, lambda: x_sb[:, :])

        def emit_qk():
            with nc.sbuf_tensor("wqk", [128, 2 * DIM], BF16) as wbuf, \
                 nc.psum_tensor("pQ0", [128, TPC], F32) as pQ0, \
                 nc.psum_tensor("pQ1", [128, TPC], F32) as pQ1:
                pQ = (pQ0, pQ1)
                tdone = {}
                vdone = {}
                idx = 0
                for wsrc, dst in ((wqt, QT), (wkt, KT)):
                    wv = wsrc.rearrange("(c p) n -> p c n", p=128)
                    for co in range(NCH):
                        par = idx % 2
                        slot = wbuf[:, DIM * par:DIM * (par + 1)]
                        waits = []
                        if idx >= 2:
                            waits.append((sT, tdone[idx - 2]))
                        wsem = sW if par == 0 else sW2
                        P.op("sync", lambda e, slot=slot, wv=wv, co=co: e.dma_start(
                            slot.rearrange("p (c n) -> p c n", c=NCH),
                            wv[:, :, 128 * co:128 * (co + 1)]), waits=waits, inc=wsem, inck=16)
                        mwaits = [(wsem, wsem.n)]
                        if idx >= 2:
                            mwaits.append((sV, vdone[idx - 2]))
                        for ci in range(NCH):
                            for hseg in range(2):
                                P.op("tensor", lambda e, par=par, slot=slot, ci=ci, hseg=hseg, co=co, idx=idx: e.matmul(
                                    pQ[par][:, 512 * hseg:512 * (hseg + 1)],
                                    slot[:, 128 * ci:128 * (ci + 1)],
                                    AB[:, TPC * ci + 512 * hseg: TPC * ci + 512 * (hseg + 1)],
                                    start=(ci == 0), stop=(ci == NCH - 1)),
                                    waits=mwaits if ci == 0 and hseg == 0 else (), inc=sT)
                        tdone[idx] = sT.n
                        P.op("vector", lambda e, dst=dst, co=co, par=par: e.tensor_copy(
                            dst[:, TPC * co:TPC * (co + 1)], pQ[par][:, :]),
                            waits=[(sT, tdone[idx])], inc=sV)
                        vdone[idx] = sV.n
                        idx += 1
                P.flush(nc, "qk")

        emit_qk()
        if KDBG:
            dbg_dump("qt", dbg_qt, lambda: QT[:, :])
            dbg_dump("kt", dbg_kt, lambda: KT[:, :])

        def emit_v():
            with nc.sbuf_tensor("wvb", [128, 1024], BF16) as wvb, \
                 nc.sbuf_tensor("vtk", [128, 2 * DIM], BF16) as vtk, \
                 nc.psum_tensor("pV", [128, 8 * 512], F32) as pV:
                t_mark = {}
                v_evac = {}
                g_out = {}
                for q in range(4):
                    for ci in range(NCH):
                        par = ci % 2
                        slot = wvb[:, 512 * par:512 * (par + 1)]
                        waits = []
                        if (q, ci - 2) in t_mark:
                            waits.append((sT, t_mark[(q, ci - 2)]))
                        elif q > 0 and ci < 2:
                            waits.append((sT, t_mark[(q - 1, NCH - 2 + ci)]))
                        wsem = sW if par == 0 else sWg
                        weng = "sync" if par == 0 else "gpsimd"
                        P.op(weng, lambda e, slot=slot, ci=ci, q=q: e.dma_start(
                            slot, wvt[128 * ci:128 * (ci + 1), 512 * q:512 * (q + 1)]),
                            waits=waits, inc=wsem, inck=16)
                        mwaits = [(wsem, wsem.n)]
                        if q > 0 and ci == 0:
                            mwaits.append((sV, v_evac[q - 1]))
                        for tb in range(8):
                            P.op("tensor", lambda e, slot=slot, ci=ci, tb=tb: e.matmul(
                                pV[:, 512 * tb:512 * (tb + 1)],
                                AB[:, TPC * ci + 128 * tb:TPC * ci + 128 * (tb + 1)],
                                slot,
                                start=(ci == 0), stop=(ci == NCH - 1), skip_group_check=True),
                                waits=mwaits if tb == 0 else (), inc=sT)
                        t_mark[(q, ci)] = sT.n
                    t_q = sT.n
                    for tb in range(8):
                        waits = [(sT, t_q)]
                        if q > 0 and tb == 0:
                            waits.append((sG, g_out[q - 1]))
                        P.op("vector", lambda e, tb=tb, q=q: e.tensor_copy(
                            vtk[:, 512 * tb:512 * (tb + 1)],
                            pV[:, 512 * tb:512 * (tb + 1)]), waits=waits, inc=sV)
                    v_evac[q] = sV.n
                    P.op("scalar", lambda e, q=q: e.dma_start(
                        Vd.rearrange("(tb p) c -> p tb c", tb=8, p=128)[:, :, 512 * q:512 * (q + 1)],
                        vtk[:, :].rearrange("p (tb n) -> p tb n", tb=8, n=512)),
                        waits=[(sV, v_evac[q])], inc=sG, inck=16)
                    g_out[q] = sG.n
                P.flush(nc, "v")
        emit_v()
        g_vdram = sG.n
        if KDBG:
            dbg_dump("vd", dbg_vd, lambda: Vd, waits=[(sG, g_vdram)])

        def emit_att():
            with nc.psum_tensor("PA", [128, 2048], F32) as PA, \
                 nc.psum_tensor("PB", [128, 2048], F32) as PB, \
                 nc.sbuf_tensor("Rb", [128, 2048], BF16) as Rb2, \
                 nc.sbuf_tensor("LKb", [128, 2048], BF16) as LK2, \
                 nc.sbuf_tensor("E0s", [32, 1024], BF16) as E0s, \
                 nc.sbuf_tensor("E1s", [32, 1024], BF16) as E1s, \
                 nc.sbuf_tensor("Zb0s", [32, 1024], BF16) as Zb0s, \
                 nc.sbuf_tensor("Zb1s", [32, 1024], BF16) as Zb1s, \
                 nc.sbuf_tensor("Zrb", [2, 1024], BF16) as Zrb, \
                 nc.sbuf_tensor("V8b", [128, 1024], BF16) as V8b2, \
                 nc.sbuf_tensor("RBD", [128, 1024], BF16) as RBD2, \
                 nc.sbuf_tensor("OUTsb", [128, 1024], F32) as OUTsb, \
                 nc.sbuf_tensor("sqatt", [128, 1024], BF16) as sqb2, \
                 nc.sbuf_tensor("rstdb", [1, 1024], BF16) as rstdb, \
                 nc.sbuf_tensor("lnmsb", [1, 1024], BF16) as lnmsb, \
                 nc.sbuf_tensor("msb", [1, 1024], BF16) as msb, \
                 nc.sbuf_tensor("Ztl", [2, 1024], BF16) as Ztl:
                P.op("gpsimd", lambda e: e.memset(RBD2[:, :], 0.0), waits=[(sG, g_vdram)], inc=sGc)
                g_z = sGc.n
                P.op("vector", lambda e: e.memset(LK2[:, :], 0.0), waits=[(sG, g_vdram)], inc=sV)
                P.op("gpsimd", lambda e: e.tensor_tensor(lam_s[:, :], lam_e[0:1, 0:1], lam_e[0:1, 1:2], ALU.subtract),
                     waits=[(sC, c_lam_exp)], inc=sGc)
                P.op("gpsimd", lambda e: e.tensor_tensor(lam_s[:, :], lam_s[:, :], cvf[0:1, 7:8], ALU.add), inc=sGc)
                P.op("gpsimd", lambda e: e.memset(lamdiag[:, :], 0.0), inc=sGc)
                v_l2 = sGc.n
                P.op("gpsimd", lambda e: e.dma_start(lamdiag[0:1, 0:1], lam_s[:, :]), waits=[(sGc, v_l2)], inc=sG, inck=16)
                P.op("gpsimd", lambda e: e.dma_start(lamdiag[1:2, 1:2], lam_s[:, :]), inc=sG, inck=16)
                g_l2 = sG.n
                P.op("tensor", lambda e: e.matmul(PB[0:2, 1024:1056], lamdiag[:, :], sel2b[:, :], start=True, stop=True),
                     waits=[(sG, g_l2)], inc=sT)
                t_l2 = sT.n
                P.op("vector", lambda e: e.tensor_copy(sel2l[:, :], PB[0:2, 1024:1056]), waits=[(sT, t_l2)], inc=sV)
                v_sel2l = sV.n

                QTv = QT[:, :].rearrange("p (c q u s) -> p q s c u", c=NCH, q=16, u=32, s=2)
                KTv = KT[:, :].rearrange("p (c q u s) -> p q s c u", c=NCH, q=16, u=32, s=2)
                Vdv = Vd.rearrange("(q g t) (h d) -> q (t h) g d", q=16, g=8, t=8, h=16)
                Ddw = Dd.rearrange("(q u s) (h a) -> q s h u a", q=16, u=32, s=2, h=16)
                Ddr = Dd.rearrange("(q g s) (h a) -> q s h g a", q=16, g=8, s=8, h=16)
                E0v = E0s[:, :].rearrange("p (u c) -> p u c", u=32)
                E1v = E1s[:, :].rearrange("p (u c) -> p u c", u=32)
                OUTv = OUTsb[:, :].rearrange("p (g t a) -> p a g t", g=8, t=8)
                RBv = PB[0:128, 1024:2048].rearrange("p (g t a) -> p a g t", g=8, t=8)
                ABv = AB[:, :].rearrange("p (a q g t) -> p q a g t", a=16, q=16, g=8, t=8)

                hist = {}

                def front(ht):
                    par = ht % 2
                    Rb = Rb2[:, 1024 * par:1024 * (par + 1)]
                    LKb = LK2[:, :]
                    Rdst = Rb.rearrange("p (u c s) -> p s c u", u=32, c=16, s=2)
                    LKdst = LKb.rearrange("p (u e t c s) -> p e t s c u", u=32, e=2, t=2, c=8, s=2)
                    Dv = Rb[0:32, 0:512].rearrange("p (u a) -> p u a", u=32)
                    h2 = hist.get(ht - 2, {})
                    h1 = hist.get(ht - 1, {})
                    cur = {}
                    hist[ht] = cur
                    bwaits = [(sT, h2.get("t_att", 0)), (sG3, h2.get("g_d", 0)), (sG, g_vdram)]
                    lkwaits = [(sT, h1.get("t_att", 0))]
                    for s1 in range(2):
                        for tau in range(2):
                            P.op("vector", lambda e, s1=s1, tau=tau, ht=ht, Rdst=Rdst: e.tensor_copy(
                                Rdst[64 * tau:64 * tau + 64, s1, :, :], QTv[64 * s1:64 * s1 + 64, ht, tau, :, :]),
                                waits=bwaits, inc=sV)
                    for sl in range(2):
                        for eh in range(2):
                            for tau in range(2):
                                P.op("vector", lambda e, sl=sl, eh=eh, tau=tau, ht=ht, LKdst=LKdst: e.tensor_copy(
                                    LKdst[64 * tau:64 * tau + 64, eh, tau, sl, :, :],
                                    KTv[64 * sl:64 * sl + 64, ht, tau, 8 * eh:8 * eh + 8, :]),
                                    waits=lkwaits, inc=sV)
                    cur["v_build"] = sV.n
                    awaits = [(sV, cur["v_build"]), (sV, h1.get("v_zb", v_sel2l))]
                    for u in range(32):
                        P.op("tensor", lambda e, u=u, LKb=LKb, Rb=Rb: e.matmul(
                            PA[0:32, 32 * u:32 * (u + 1)], LKb[:, 64 * u:64 * u + 32],
                            Rb[:, 32 * u:32 * (u + 1)], start=True, stop=True),
                            waits=awaits if u == 0 else (), inc=sT)
                        P.op("tensor", lambda e, u=u, LKb=LKb, Rb=Rb: e.matmul(
                            PA[0:32, 1024 + 32 * u:1024 + 32 * (u + 1)], LKb[:, 64 * u + 32:64 * u + 64],
                            Rb[:, 32 * u:32 * (u + 1)], start=True, stop=True), inc=sT)
                    cur["t_att"] = sT.n
                    P.op("scalar", lambda e: e.activation(E0s[:, :], PA[0:32, 0:1024], AF.Exp, scale=SCALE),
                         waits=[(sT, cur["t_att"]), (sV, h1.get("v_d", 0)), (sT, h1.get("t_z", 0))], inc=sC)
                    P.op("scalar", lambda e: e.activation(E1s[:, :], PA[0:32, 1024:2048], AF.Exp, scale=SCALE),
                         inc=sC)
                    cur["c_exp"] = sC.n

                def mid(ht):
                    par = ht % 2
                    Rb = Rb2[:, 1024 * par:1024 * (par + 1)]
                    Dv = Rb[0:32, 0:512].rearrange("p (u a) -> p u a", u=32)
                    h1 = hist.get(ht - 1, {})
                    cur = hist[ht]
                    P.op("gpsimd", lambda e, ht=ht: e.dma_start(
                        V8b2[:, :].rearrange("p (g d) -> p g d", g=8), Vdv[ht]),
                        waits=[(sG, g_vdram), (sT, h1.get("t_out", 0))], inc=sG2, inck=16)
                    cur["g_v8"] = sG2.n
                    for hseg in range(2):
                        for srce, st_, sp_ in ((E0s, True, False), (E1s, False, True)):
                            P.op("tensor", lambda e, srce=srce, st_=st_, sp_=sp_, hseg=hseg: e.matmul(
                                PB[0:2, 512 * hseg:512 * (hseg + 1)], b2a[:, :],
                                srce[:, 512 * hseg:512 * (hseg + 1)],
                                start=st_, stop=sp_),
                                waits=[(sC, cur["c_exp"]), (sV, h1.get("v_sq", 0))] if st_ and hseg == 0 else (),
                                inc=sT)
                    cur["t_z"] = sT.n
                    P.op("scalar", lambda e: e.activation(Ztl[:, :], PB[0:2, 0:1024], AF.Ln),
                         waits=[(sT, cur["t_z"])], inc=sC)
                    cur["c_zl"] = sC.n
                    P.op("scalar", lambda e: e.activation(Zrb[:, :], Ztl[:, :], AF.Exp, scale=-1.0),
                         waits=[(sT, h1.get("t_zb", 0))], inc=sC)
                    cur["c_zr"] = sC.n
                    for lhs, coloff in ((sel2b, 0), (sel2l, 1024)):
                        for hseg in range(2):
                            P.op("tensor", lambda e, lhs=lhs, coloff=coloff, hseg=hseg: e.matmul(
                                PA[0:32, coloff + 512 * hseg:coloff + 512 * (hseg + 1)], lhs[:, :],
                                Zrb[0:2, 512 * hseg:512 * (hseg + 1)], start=True, stop=True),
                                waits=[(sC, cur["c_zr"]), (sV, v_sel2l)] if (coloff == 0 and hseg == 0) else (),
                                inc=sT)
                    cur["t_zb"] = sT.n
                    P.op("vector", lambda e: e.tensor_copy(Zb0s[:, :], PA[0:32, 0:1024]),
                         waits=[(sT, cur["t_zb"])], inc=sV)
                    P.op("vector", lambda e: e.tensor_copy(Zb1s[:, :], PA[0:32, 1024:2048]), inc=sV)
                    cur["v_zb"] = sV.n
                    P.op("vector", lambda e: e.tensor_tensor(E0s[:, :], E0s[:, :], Zb0s[:, :], ALU.mult), inc=sV)
                    P.op("vector", lambda e: e.tensor_tensor(E1s[:, :], E1s[:, :], Zb1s[:, :], ALU.mult), inc=sV)
                    P.op("vector", lambda e, Dv=Dv: e.tensor_tensor(Dv, E0v[:, :, 0:16], E1v[:, :, 16:32], ALU.subtract),
                         inc=sV)
                    cur["v_d"] = sV.n
                    for tau2 in range(2):
                        P.op("gpsimd", lambda e, tau2=tau2, ht=ht, Rb=Rb: e.dma_start(
                            Ddw[ht, tau2],
                            Rb[16 * tau2:16 * tau2 + 16, 0:512].rearrange("p (u a) -> p u a", u=32)),
                            waits=[(sV, cur["v_d"])], inc=sG3, inck=16)
                    cur["g_d"] = sG3.n
                    rsem = sS if ht % 2 == 0 else sS2
                    for tau in range(8):
                        P.op("sync", lambda e, tau=tau, ht=ht: e.dma_start(
                            RBD2[16 * tau:16 * tau + 16, :].rearrange("p (g c) -> p g c", g=8)[:, :, 16 * tau:16 * tau + 16],
                            Ddr[ht, tau]),
                            waits=[(sG3, cur["g_d"]), (sT, h1.get("t_out", 0)), (sGc, g_z)], inc=rsem, inck=16)
                    cur["s_rbd"] = (rsem, rsem.n)

                def back(ht):
                    if ht < 0:
                        return
                    cur = hist[ht]
                    h1 = hist.get(ht - 1, {})
                    owaits = [cur["s_rbd"], (sG2, cur["g_v8"]), (sC, cur["c_zl"])]
                    for g in range(8):
                        P.op("tensor", lambda e, g=g: e.matmul(
                            PB[0:128, 128 * g:128 * (g + 1)], V8b2[:, 128 * g:128 * (g + 1)],
                            RBD2[:, 128 * g:128 * (g + 1)], start=True, stop=True),
                            waits=owaits if g == 0 else (), inc=sT)
                    cur["t_out"] = sT.n
                    P.op("vector", lambda e: e.tensor_copy(OUTsb[:, :], PB[0:128, 0:1024]),
                         waits=[(sT, cur["t_out"]), (sC, cur["c_zl"])], inc=sV)
                    P.op("vector", lambda e: e.tensor_tensor(sqb2[:, :], OUTsb[:, :], OUTsb[:, :], ALU.mult),
                         inc=sV)
                    cur["v_sq"] = sV.n
                    cur["c_sq"] = sC.n
                    for hseg in range(2):
                        P.op("tensor", lambda e, hseg=hseg: e.matmul(
                            PB[0:1, 1024 + 512 * hseg:1024 + 512 * (hseg + 1)], o128b[:, :],
                            sqb2[:, 512 * hseg:512 * (hseg + 1)], start=True, stop=True),
                            waits=[(sV, cur["v_sq"]), (sV, h1.get("v_apply", 0)),
                                   (sC, h1.get("c_rs", 0))] if hseg == 0 else (), inc=sT)
                    cur["t_ms"] = sT.n
                    P.op("vector", lambda e: e.tensor_single_scalar(msb[:, :], PB[0:1, 1024:2048], 128.0 * EPS, ALU.add),
                         waits=[(sT, cur["t_ms"])], inc=sV)
                    cur["v_msb"] = sV.n
                    P.op("scalar", lambda e: e.activation(lnmsb[:, :], msb[:, :], AF.Ln),
                         waits=[(sV, cur["v_msb"])], inc=sC)
                    cur["c_ml"] = sC.n
                    P.op("scalar", lambda e: e.activation(rstdb[:, :], lnmsb[:, :], AF.Exp, scale=-0.5),
                         waits=[(sT, h1.get("t_rb", 0))], inc=sC)
                    cur["c_rs"] = sC.n
                    for hseg in range(2):
                        P.op("tensor", lambda e, hseg=hseg: e.matmul(
                            PB[0:128, 1024 + 512 * hseg:1024 + 512 * (hseg + 1)], o128br[:, :],
                            rstdb[0:1, 512 * hseg:512 * (hseg + 1)], start=True, stop=True),
                            waits=[(sC, cur["c_rs"])] if hseg == 0 else (), inc=sT)
                    cur["t_rb"] = sT.n
                    P.op("vector", lambda e, ht=ht: e.tensor_tensor(
                        ABv[:, ht], OUTv[:, :, :, :], RBv[:, :, :, :], ALU.mult),
                        waits=[(sT, cur["t_rb"])], inc=sV)
                    cur["v_apply"] = sV.n

                for ht in range(16):
                    front(ht)
                    back(ht - 1)
                    mid(ht)
                back(15)
                if KDBG2:
                    fin = hist[15]
                    P.op("gpsimd", lambda e: e.dma_start(dbg_es2[:, 0:1024], E0s[:, :]),
                         waits=[(sV, fin["v_d"])], inc=sG, inck=16)
                    P.op("gpsimd", lambda e: e.dma_start(dbg_es2[:, 1024:2048], E1s[:, :]), inc=sG, inck=16)
                    P.op("gpsimd", lambda e: e.dma_start(dbg_rb2x, Rb2[:, :]), inc=sG, inck=16)
                    P.op("gpsimd", lambda e: e.dma_start(dbg_lk2x, LK2[:, :]), inc=sG, inck=16)
                    P.op("gpsimd", lambda e: e.dma_start(dbg_zb2x[:, 0:1024], Zb0s[:, :]), inc=sG, inck=16)
                    P.op("gpsimd", lambda e: e.dma_start(dbg_zb2x[:, 1024:2048], Zb1s[:, :]), inc=sG, inck=16)
                    P.op("gpsimd", lambda e: e.dma_start(dbg_kt2, KT[:, :]), inc=sG, inck=16)
                    P.op("gpsimd", lambda e: e.dma_start(dbg_qt2, QT[:, :]), inc=sG, inck=16)
                P.flush(nc, "att")

        emit_att()
        if KDBG2:
            P.op("gpsimd", lambda e: e.dma_start(dbg_vd2, Vd), inc=sG, inck=16)
            P.op("gpsimd", lambda e: e.dma_start(dbg_lam2, lam_s[:, :]), inc=sG, inck=16)
            P.op("gpsimd", lambda e: e.dma_start(dbg_sel2, sel2l[:, :]), inc=sG, inck=16)
            P.op("gpsimd", lambda e: e.dma_start(dbg_dd2, Dd), inc=sG, inck=16)
            for nm_, sb_ in (("dbg_rbd2", "RBD2"), ("dbg_v82", "V8b2"), ("dbg_osb2", "OUTsb"), ("dbg_rst2", "rstdb")):
                pass
            P.op("gpsimd", lambda e: e.dma_start(dbg_outn2, AB[:, :]), inc=sG, inck=16)
            gw2 = sG.n
            P.op("gpsimd", lambda e: e.nop(), waits=[(sG, gw2)])
            P.flush(nc, "dbg2")
        if KDBG:
            dbg_dump("outn", dbg_outn, lambda: AB[:, :])

        def emit_wo():
            wov = wot.rearrange("(c p) n -> p c n", p=128)
            with nc.psum_tensor("pW0", [128, TPC], F32) as pW0, \
                 nc.psum_tensor("pW1", [128, TPC], F32) as pW1:
                pW = (pW0, pW1)
                tdone = {}
                vres = {}
                for co in range(NCH):
                    par = co % 2
                    slot = KT[:, DIM * par:DIM * (par + 1)]
                    waits = []
                    if co >= 2:
                        waits.append((sT, tdone[co - 2]))
                    wsem = sW if par == 0 else sW2
                    P.op("sync", lambda e, slot=slot, co=co: e.dma_start(
                        slot.rearrange("p (c n) -> p c n", c=NCH),
                        wov[:, :, 128 * co:128 * (co + 1)]), waits=waits, inc=wsem, inck=16)
                    w_here = wsem.n
                    mwaits = [(wsem, w_here)]
                    if co >= 2:
                        mwaits.append((sV, vres[co - 2]))
                    nm = 0
                    for a in range(NCH):
                        for hseg in range(2):
                            P.op("tensor", lambda e, par=par, slot=slot, a=a, hseg=hseg: e.matmul(
                                pW[par][:, 512 * hseg:512 * (hseg + 1)],
                                slot[:, 128 * a:128 * (a + 1)],
                                AB[:, TPC * a + 512 * hseg:TPC * a + 512 * (hseg + 1)],
                                start=(a == 0), stop=False),
                                waits=mwaits if nm == 0 else (), inc=sT)
                            nm += 1
                    for hseg in range(2):
                        P.op("tensor", lambda e, par=par, co=co, hseg=hseg: e.matmul(
                            pW[par][:, 512 * hseg:512 * (hseg + 1)],
                            bo_sb[0:1, 128 * co:128 * (co + 1)],
                            oNb[0:1, 512 * hseg:512 * (hseg + 1)],
                            start=False, stop=True), inc=sT)
                    tdone[co] = sT.n
                    P.op("vector", lambda e, co=co, par=par: e.tensor_tensor(
                        x_sb[:, TPC * co:TPC * (co + 1)], x_sb[:, TPC * co:TPC * (co + 1)],
                        pW[par][:, :], ALU.add), waits=[(sT, tdone[co])], inc=sV)
                    vres[co] = sV.n
                P.flush(nc, "wo")

        emit_wo()
        if KDBG:
            dbg_dump("x2", dbg_x2, lambda: x_sb[:, :])
        emit_ln2("2")
        if KDBG:
            dbg_dump("ln2", dbg_ln2, lambda: AB[:, :])

        def emit_ffn():
            wgv = wgp.rearrange("(c p) n -> p c n", p=128)
            wuv = wup.rearrange("(c p) n -> p c n", p=128)
            wdv = wdp.rearrange("(m b j p) n -> b p m j n", m=4, b=4, j=4, p=128)
            with nc.psum_tensor("PC1", [128, 2048], F32) as PC1, \
                 nc.psum_tensor("PC2", [128, 2048], F32) as PC2, \
                 nc.sbuf_tensor("Gs", [128, TPC], F32) as Gs, \
                 nc.sbuf_tensor("q4", [128, 4 * TPC], BF16) as q4, \
                 nc.sbuf_tensor("wins", [128, 4 * TPC], BF16) as wins, \
                 nc.sbuf_tensor("tbuf", [128, TPC], BF16) as tbuf:
                PC = (PC1, PC2)
                wg_slot = lambda par: KT[:, 2048 * par:2048 * (par + 1)]
                wu_slot = lambda par: KT[:, 4096 + 2048 * par:4096 + 2048 * (par + 1)]
                wd_slot = lambda par: KT[:, 8192 + 2048 * par:8192 + 2048 * (par + 1)]
                t_gu = {}
                c_gu = {}
                v_h = {}
                g_sq = {}
                v_mask = 0
                t_down_prev = 0
                v_yadd = {}
                prev_q_applied = 0
                cidx = 0
                for q in range(4):
                    v_mask_j = {}
                    for j in range(4):
                        for m in range(4):
                            gidx = 4 * q + j + 16 * m
                            s = 4 * m + j
                            par = cidx % 2
                            waits = []
                            if cidx >= 2:
                                waits.append((sT, t_gu.get(cidx - 2, 0)))
                            wsem = sW if par == 0 else sW2
                            P.op("sync", lambda e, par=par, gidx=gidx: e.dma_start(
                                wg_slot(par).rearrange("p (c n) -> p c n", c=NCH),
                                wgv[:, :, 128 * gidx:128 * (gidx + 1)]), waits=waits, inc=wsem, inck=16)
                            P.op("sync", lambda e, par=par, gidx=gidx: e.dma_start(
                                wu_slot(par).rearrange("p (c n) -> p c n", c=NCH),
                                wuv[:, :, 128 * gidx:128 * (gidx + 1)]), inc=wsem, inck=16)
                            mwaits = [(wsem, wsem.n)]
                            if cidx >= 2:
                                mwaits.append((sC, c_gu.get(cidx - 2, 0)))
                                mwaits.append((sV, v_h.get(cidx - 2, 0)))
                            if q > 0 and cidx % 16 < 2:
                                mwaits.append((sV, v_yadd.get((q - 1, 15 - (1 - cidx % 16)), 0)))
                            nm = 0
                            for ci in range(NCH):
                                for hseg in range(2):
                                    P.op("tensor", lambda e, par=par, ci=ci, hseg=hseg: e.matmul(
                                        PC[par][:, 512 * hseg:512 * (hseg + 1)],
                                        wg_slot(par)[:, 128 * ci:128 * (ci + 1)],
                                        AB[:, TPC * ci + 512 * hseg:TPC * ci + 512 * (hseg + 1)],
                                        start=(ci == 0), stop=(ci == NCH - 1)),
                                        waits=mwaits if nm == 0 else (), inc=sT)
                                    nm += 1
                            for ci in range(NCH):
                                for hseg in range(2):
                                    P.op("tensor", lambda e, par=par, ci=ci, hseg=hseg: e.matmul(
                                        PC[par][:, 1024 + 512 * hseg:1024 + 512 * (hseg + 1)],
                                        wu_slot(par)[:, 128 * ci:128 * (ci + 1)],
                                        AB[:, TPC * ci + 512 * hseg:TPC * ci + 512 * (hseg + 1)],
                                        start=(ci == 0), stop=(ci == NCH - 1)), inc=sT)
                            t_gu[cidx] = sT.n
                            P.op("scalar", lambda e, par=par: e.activation(
                                Gs[:, :], PC[par][:, 0:1024], AF.Silu),
                                waits=[(sT, t_gu[cidx]), (sV, v_h.get(cidx - 1, 0))], inc=sC)
                            c_gu[cidx] = sC.n
                            hwaits = [(sC, c_gu[cidx])]
                            if q > 0:
                                hwaits.append((sT, t_down_prev))
                            P.op("vector", lambda e, par=par, s=s: e.tensor_tensor(
                                QT[:, TPC * s:TPC * (s + 1)], Gs[:, :], PC[par][:, 1024:2048], ALU.mult),
                                waits=hwaits, inc=sV)
                            v_h[cidx] = sV.n
                            P.op("vector", lambda e, m=m, s=s: e.tensor_tensor(
                                q4[:, TPC * m:TPC * (m + 1)], QT[:, TPC * s:TPC * (s + 1)],
                                QT[:, TPC * s:TPC * (s + 1)], ALU.mult),
                                waits=[(sV, v_h[cidx])], inc=sV)
                            g_sq[(q, j, m)] = sV.n
                            cidx += 1
                        g_here = sV.n
                        qs = [q4[:, TPC * m:TPC * (m + 1)] for m in range(4)]
                        ws = [wins[:, TPC * m:TPC * (m + 1)] for m in range(4)]
                        mw = []
                        def vop(build, waits=()):
                            P.op("vector", build, waits=waits, inc=sV)
                        vop(lambda e: e.tensor_tensor(ws[1], qs[0], qs[1], ALU.is_ge), waits=mw)
                        vop(lambda e: e.tensor_copy(ws[0], ws[1]))
                        vop(lambda e: e.tensor_tensor(ws[2], qs[0], qs[2], ALU.is_ge))
                        vop(lambda e: e.tensor_tensor(ws[0], ws[0], ws[2], ALU.add))
                        vop(lambda e: e.tensor_tensor(ws[3], qs[0], qs[3], ALU.is_ge))
                        vop(lambda e: e.tensor_tensor(ws[0], ws[0], ws[3], ALU.add))
                        vop(lambda e: e.tensor_tensor(tbuf[:, :], qs[1], qs[2], ALU.is_ge))
                        vop(lambda e: e.tensor_tensor(ws[1], ws[1], tbuf[:, :], ALU.subtract))
                        vop(lambda e: e.tensor_tensor(ws[2], ws[2], tbuf[:, :], ALU.add))
                        vop(lambda e: e.tensor_tensor(tbuf[:, :], qs[1], qs[3], ALU.is_ge))
                        vop(lambda e: e.tensor_tensor(ws[1], ws[1], tbuf[:, :], ALU.subtract))
                        vop(lambda e: e.tensor_tensor(ws[3], ws[3], tbuf[:, :], ALU.add))
                        vop(lambda e: e.tensor_tensor(tbuf[:, :], qs[2], qs[3], ALU.is_ge))
                        vop(lambda e: e.tensor_tensor(ws[2], ws[2], tbuf[:, :], ALU.subtract))
                        vop(lambda e: e.tensor_tensor(ws[3], ws[3], tbuf[:, :], ALU.add))
                        vop(lambda e: e.tensor_single_scalar(ws[0], ws[0], 1.5, ALU.is_ge))
                        vop(lambda e: e.tensor_single_scalar(ws[1], ws[1], -0.6, ALU.is_le))
                        vop(lambda e: e.tensor_single_scalar(ws[2], ws[2], 0.4, ALU.is_le))
                        vop(lambda e: e.tensor_single_scalar(ws[3], ws[3], 1.4, ALU.is_le))
                        for m in range(4):
                            s = 4 * m + j
                            vop(lambda e, m=m, s=s: e.tensor_tensor(
                                QT[:, TPC * s:TPC * (s + 1)], QT[:, TPC * s:TPC * (s + 1)],
                                wins[:, TPC * m:TPC * (m + 1)], ALU.mult))
                        v_mask = sV.n
                        v_mask_j[j] = sV.n
                    tdd = {}
                    for co in range(NCH):
                        par = co % 2
                        waits = [(sT, t_gu[cidx - 1])]
                        if co >= 2:
                            waits = [(sT, tdd[co - 2])]
                        wsem = sWd if par == 0 else sWd2
                        for mm_ in range(4):
                            P.op("sync", lambda e, par=par, q=q, co=co, mm_=mm_: e.dma_start(
                                wd_slot(par).rearrange("p (m j n) -> p m j n", m=4, j=4)[:, mm_],
                                wdv[q][:, mm_, :, 128 * co:128 * (co + 1)]),
                                waits=waits if mm_ == 0 else (), inc=wsem, inck=16)
                        mwaits = [(wsem, wsem.n), (sV, v_mask_j[3])]
                        if co >= 2:
                            mwaits.append((sV, v_yadd[(q, co - 2)]))
                        nm = 0
                        for s in range(16):
                            for hseg in range(2):
                                P.op("tensor", lambda e, par=par, s=s, hseg=hseg: e.matmul(
                                    PC[par][:, 512 * hseg:512 * (hseg + 1)],
                                    wd_slot(par)[:, 128 * s:128 * (s + 1)],
                                    QT[:, TPC * s + 512 * hseg:TPC * s + 512 * (hseg + 1)],
                                    start=(s == 0), stop=(s == 15)),
                                    waits=mwaits if nm == 0 else (), inc=sT)
                                nm += 1
                        tdd[co] = sT.n
                        P.op("vector", lambda e, co=co, par=par: e.tensor_tensor(
                            x_sb[:, TPC * co:TPC * (co + 1)], x_sb[:, TPC * co:TPC * (co + 1)],
                            PC[par][:, 0:1024], ALU.add), waits=[(sT, tdd[co])], inc=sV)
                        v_yadd[(q, co)] = sV.n
                        if q == 3:
                            P.op("gpsimd", lambda e, co=co: e.dma_start(
                                y_d[128 * co:128 * (co + 1), :], x_sb[:, TPC * co:TPC * (co + 1)]),
                                waits=[(sV, v_yadd[(q, co)])], inc=sG, inck=16)
                    t_down_prev = sT.n
                g_final = sG.n
                P.op("gpsimd", lambda e: e.nop(), waits=[(sG, g_final)])
                P.flush(nc, "ffn")

        emit_ffn()

    stk.close()
    return nc


def _prep_inputs(inputs):
    permQ = np.empty(DIM, np.int64)
    for s in range(2):
        for h in range(H):
            for d in range(DH):
                permQ[(s * H + h) * DH + d] = h * 128 + 2 * d + s
    permI = np.empty(IFFN, np.int64)
    for g in range(2048):
        for m in range(4):
            permI[m * 2048 + g] = 4 * g + m
    b16 = lambda a: np.ascontiguousarray(a).astype(BD)
    f32 = lambda a: np.ascontiguousarray(a, dtype=np.float32)
    common = {
        "wqt": b16(inputs["L0_attn_Wq"][permQ, :].T),
        "wkt": b16(inputs["L0_attn_Wk"][permQ, :].T),
        "wvt": b16(inputs["L0_attn_Wv"].T),
        "wot": b16(inputs["L0_attn_Wo"].T),
        "wgp": b16(inputs["L0_ffn_w_gate"][:, permI]),
        "wup": b16(inputs["L0_ffn_w_up"][:, permI]),
        "wdp": b16(inputs["L0_ffn_w_down"][permI, :]),
        "bo": b16(inputs["L0_attn_bo"].reshape(1, DIM)),
        "lqk": f32(np.concatenate([inputs["L0_attn_lq1"], inputs["L0_attn_lq2"],
                                   inputs["L0_attn_lk1"], inputs["L0_attn_lk2"]]).reshape(1, 256)),
    }
    b2a = np.zeros((32, 2), np.float32)
    for k in range(32):
        b2a[k, k // 16] = 1.0
    sel2 = np.zeros((2, 32), np.float32)
    for r in range(32):
        sel2[r // 16, r] = 1.0
    common["b2a"] = b16(b2a)
    common["sel2b"] = b16(sel2)
    common["sel2f"] = f32(sel2)
    common["o128b"] = b16(np.ones((128, 1), np.float32))
    common["o128f"] = f32(np.ones((1, 128), np.float32))
    common["o128br"] = b16(np.full((1, 128), (1.0 - LAM_INIT) * float(np.sqrt(128.0)), np.float32))
    common["o2f"] = f32(np.ones((1, 2), np.float32))
    common["oNb"] = b16(np.ones((1, TPC), np.float32))
    common["cvf"] = f32(np.concatenate([np.array(CONSTS_F32, np.float64),
                                        np.zeros(128 - len(CONSTS_F32))]).reshape(1, -1))
    common["cvb"] = np.concatenate([np.array(CONSTS_B16, np.float32),
                                    np.zeros(256 - len(CONSTS_B16), np.float32)]).astype(BD).reshape(1, -1)
    x = np.asarray(inputs["L0_attn_x"], dtype=np.float32).reshape(8192, DIM)
    in_maps = []
    for c in range(NCORES):
        m = dict(common)
        m["xT"] = np.ascontiguousarray(x[TPC * c:TPC * (c + 1), :].T)
        in_maps.append(m)
    return in_maps


def kernel(**inputs):
    in_maps = _prep_inputs(inputs)
    nc = build_program()
    res = run_bass_kernel_spmd(nc, in_maps, core_ids=list(range(NCORES)))
    out = np.empty((8192, DIM), np.float32)
    for c in range(NCORES):
        out[TPC * c:TPC * (c + 1), :] = res.results[c]["y"].T
    return out.reshape(1, 8192, DIM)




# version 31
# speedup vs baseline: 1.0093x; 1.0093x over previous
import os
import numpy as np
import ml_dtypes
from contextlib import ExitStack

KDBG = bool(int(os.environ.get("KDBG", "0")))
KDBG2 = bool(int(os.environ.get("KDBG2", "0")))
SELFSYNC = os.environ.get("KSELF", "vector,scalar,gpsimd")

import concourse.bass as bass
import concourse.mybir as mybir
from concourse.bass_utils import run_bass_kernel_spmd

F32 = mybir.dt.float32
BF16 = mybir.dt.bfloat16
AF = mybir.ActivationFunctionType
ALU = mybir.AluOpType
BD = ml_dtypes.bfloat16

DIM = 2048
H = 16
DH = 64
NCH = 16
TPC = 1024
NCORES = 8
IFFN = 8192
LAYER_NUM = 12
LAM_INIT = 0.8 - 0.6 * float(np.exp(-0.3 * LAYER_NUM))
SCALE = DH ** -0.5
EPS = 1e-5
RMS_BIAS = float(np.log(1.0 - LAM_INIT))

CONSTS_F32 = [0.125, -1.0, -0.5, 1e-5, 1.0 / 128.0, RMS_BIAS, 1.0 / DIM, float(LAM_INIT), 128.0 * EPS,
              1.5, 0.5, 1.0, 2.0]
CONSTS_B16 = [-1.0, 1.0, 1.5, 0.5, -0.5, 0.125, 2.0, 0.0]

ENGS = ("sync", "gpsimd", "tensor", "scalar", "vector")


class Sem:
    def __init__(self, stk, nc, name):
        self.h = stk.enter_context(nc.semaphore(name))
        self.n = 0
        self.name = name


class Prog:

    def __init__(self):
        self.streams = {e: [] for e in ENGS}
        self.waited = {}
        self.chain = {}

    def op(self, eng, build, waits=(), inc=None, inck=1):
        ch = self.chain.get(eng)
        if ch is not None:
            last = getattr(self, "_last_done", {}).get(eng)
            if last is not None:
                waits = list(waits) + [last]
        wl = []
        for s, v in waits:
            key = (eng, s.name)
            if v > self.waited.get(key, -1):
                wl.append((s.h, v))
                self.waited[key] = v
        ih = None
        if inc is not None:
            ih = (inc.h, inck)
            inc.n += inck
        ch2 = None
        if ch is not None:
            if not hasattr(self, "_last_done"):
                self._last_done = {}
            if inc is not None:
                self._last_done[eng] = (inc, inc.n)
            else:
                k = 16 if inck == 16 else 1
                ch.n += k
                ch2 = (ch.h, k)
                self._last_done[eng] = (ch, ch.n)
        def thunk(e, wl=tuple(wl), build=build, ih=ih, ch2=ch2):
            for h, v in wl:
                e.wait_ge(h, v)
            ins = build(e)
            if ins is not None:
                if ih is not None:
                    ins.then_inc(ih[0], ih[1])
                if ch2 is not None:
                    ins.then_inc(ch2[0], ch2[1])
        self.streams[eng].append(thunk)
        return inc.n if inc is not None else None

    def flush(self, nc, name):
        streams = self.streams
        self.streams = {e: [] for e in ENGS}
        with nc.Block(name) as blk:
            if streams["sync"]:
                blk.sync(lambda e, s=streams["sync"]: [f(e) for f in s])
            if streams["gpsimd"]:
                blk.gpsimd(lambda e, s=streams["gpsimd"]: [f(e) for f in s])
            if streams["tensor"]:
                blk.tensor(lambda e, s=streams["tensor"]: [f(e) for f in s])
            if streams["scalar"]:
                blk.scalar(lambda e, s=streams["scalar"]: [f(e) for f in s])
            if streams["vector"]:
                blk.vector(lambda e, s=streams["vector"]: [f(e) for f in s])


def build_program():
    nc = bass.Bass("TRN2", target_bir_lowering=False)
    stk = ExitStack()

    xT = nc.dram_tensor("xT", [DIM, TPC], F32, kind="ExternalInput").ap()
    wqt = nc.dram_tensor("wqt", [DIM, DIM], BF16, kind="ExternalInput").ap()
    wkt = nc.dram_tensor("wkt", [DIM, DIM], BF16, kind="ExternalInput").ap()
    wvt = nc.dram_tensor("wvt", [DIM, DIM], BF16, kind="ExternalInput").ap()
    wot = nc.dram_tensor("wot", [DIM, DIM], BF16, kind="ExternalInput").ap()
    wgp = nc.dram_tensor("wgp", [DIM, IFFN], BF16, kind="ExternalInput").ap()
    wup = nc.dram_tensor("wup", [DIM, IFFN], BF16, kind="ExternalInput").ap()
    wdp = nc.dram_tensor("wdp", [IFFN, DIM], BF16, kind="ExternalInput").ap()
    bo_d = nc.dram_tensor("bo", [1, DIM], BF16, kind="ExternalInput").ap()
    lqk_d = nc.dram_tensor("lqk", [1, 256], F32, kind="ExternalInput").ap()
    b2a_d = nc.dram_tensor("b2a", [32, 2], BF16, kind="ExternalInput").ap()
    sel2b_d = nc.dram_tensor("sel2b", [2, 32], BF16, kind="ExternalInput").ap()
    sel2f_d = nc.dram_tensor("sel2f", [2, 32], F32, kind="ExternalInput").ap()
    o128b_d = nc.dram_tensor("o128b", [128, 1], BF16, kind="ExternalInput").ap()
    o128f_d = nc.dram_tensor("o128f", [1, 128], F32, kind="ExternalInput").ap()
    o128br_d = nc.dram_tensor("o128br", [1, 128], BF16, kind="ExternalInput").ap()
    o2f_d = nc.dram_tensor("o2f", [1, 2], F32, kind="ExternalInput").ap()
    oNb_d = nc.dram_tensor("oNb", [1, TPC], BF16, kind="ExternalInput").ap()
    cvf_d = nc.dram_tensor("cvf", [1, 128], F32, kind="ExternalInput").ap()
    cvb_d = nc.dram_tensor("cvb", [1, 256], BF16, kind="ExternalInput").ap()
    y_d = nc.dram_tensor("y", [DIM, TPC], F32, kind="ExternalOutput").ap()
    Vd = nc.dram_tensor("Vd", [TPC, DIM], BF16, kind="Internal").ap()
    Dd = nc.dram_tensor("Dd", [TPC, 256], BF16, kind="Internal").ap()
    if KDBG2:
        dbg_outn2 = nc.dram_tensor("dbg_outn2", [128, NCH * TPC], BF16, kind="ExternalOutput").ap()
        dbg_vd2 = nc.dram_tensor("dbg_vd2", [TPC, DIM], BF16, kind="ExternalOutput").ap()
        dbg_lam2 = nc.dram_tensor("dbg_lam2", [1, 1], F32, kind="ExternalOutput").ap()
        dbg_sel2 = nc.dram_tensor("dbg_sel2", [2, 32], BF16, kind="ExternalOutput").ap()
        dbg_dd2 = nc.dram_tensor("dbg_dd2", [TPC, 256], BF16, kind="ExternalOutput").ap()
        dbg_rbd2 = nc.dram_tensor("dbg_rbd2", [128, 1024], BF16, kind="ExternalOutput").ap()
        dbg_v82 = nc.dram_tensor("dbg_v82", [128, 1024], BF16, kind="ExternalOutput").ap()
        dbg_osb2 = nc.dram_tensor("dbg_osb2", [128, 1024], F32, kind="ExternalOutput").ap()
        dbg_rst2 = nc.dram_tensor("dbg_rst2", [1, 1024], BF16, kind="ExternalOutput").ap()
        dbg_es2 = nc.dram_tensor("dbg_es2", [32, 2048], BF16, kind="ExternalOutput").ap()
        dbg_rb2x = nc.dram_tensor("dbg_rb2x", [128, 2048], BF16, kind="ExternalOutput").ap()
        dbg_lk2x = nc.dram_tensor("dbg_lk2x", [128, 2048], BF16, kind="ExternalOutput").ap()
        dbg_zb2x = nc.dram_tensor("dbg_zb2x", [32, 2048], BF16, kind="ExternalOutput").ap()
        dbg_kt2 = nc.dram_tensor("dbg_kt2", [128, NCH * TPC], BF16, kind="ExternalOutput").ap()
        dbg_qt2 = nc.dram_tensor("dbg_qt2", [128, NCH * TPC], BF16, kind="ExternalOutput").ap()
    if KDBG:
        dbg_ln1 = nc.dram_tensor("dbg_ln1", [128, NCH * TPC], BF16, kind="ExternalOutput").ap()
        dbg_qt = nc.dram_tensor("dbg_qt", [128, NCH * TPC], BF16, kind="ExternalOutput").ap()
        dbg_kt = nc.dram_tensor("dbg_kt", [128, NCH * TPC], BF16, kind="ExternalOutput").ap()
        dbg_vd = nc.dram_tensor("dbg_vd", [TPC, DIM], BF16, kind="ExternalOutput").ap()
        dbg_outn = nc.dram_tensor("dbg_outn", [128, NCH * TPC], BF16, kind="ExternalOutput").ap()
        dbg_x2 = nc.dram_tensor("dbg_x2", [128, NCH * TPC], F32, kind="ExternalOutput").ap()
        dbg_e0 = nc.dram_tensor("dbg_e0", [32, 1024], BF16, kind="ExternalOutput").ap()
        dbg_e1 = nc.dram_tensor("dbg_e1", [32, 1024], BF16, kind="ExternalOutput").ap()
        dbg_zr = nc.dram_tensor("dbg_zr", [2, 1024], BF16, kind="ExternalOutput").ap()
        dbg_zb0 = nc.dram_tensor("dbg_zb0", [32, 1024], BF16, kind="ExternalOutput").ap()
        dbg_zb1 = nc.dram_tensor("dbg_zb1", [32, 1024], BF16, kind="ExternalOutput").ap()
        dbg_dm = nc.dram_tensor("dbg_dm", [32, 512], BF16, kind="ExternalOutput").ap()
        dbg_rbd = nc.dram_tensor("dbg_rbd", [128, 1024], BF16, kind="ExternalOutput").ap()
        dbg_v8 = nc.dram_tensor("dbg_v8", [128, 1024], BF16, kind="ExternalOutput").ap()
        dbg_osb = nc.dram_tensor("dbg_osb", [128, 1024], F32, kind="ExternalOutput").ap()
        dbg_rst = nc.dram_tensor("dbg_rst", [1, 1024], BF16, kind="ExternalOutput").ap()
        dbg_lnms = nc.dram_tensor("dbg_lnms", [1, 1024], F32, kind="ExternalOutput").ap()
        dbg_x0 = nc.dram_tensor("dbg_x0", [128, NCH * TPC], F32, kind="ExternalOutput").ap()
        dbg_rb = nc.dram_tensor("dbg_rb", [128, TPC], BF16, kind="ExternalOutput").ap()
        dbg_bmu = nc.dram_tensor("dbg_bmu", [128, TPC], F32, kind="ExternalOutput").ap()
        dbg_msq = nc.dram_tensor("dbg_msq", [1, TPC], F32, kind="ExternalOutput").ap()
        dbg_var = nc.dram_tensor("dbg_var", [1, TPC], F32, kind="ExternalOutput").ap()
        dbg_lnv = nc.dram_tensor("dbg_lnv", [1, TPC], F32, kind="ExternalOutput").ap()
        dbg_rsd = nc.dram_tensor("dbg_rsd", [1, TPC], F32, kind="ExternalOutput").ap()
        dbg_tb = nc.dram_tensor("dbg_tb", [128, NCH * TPC], BF16, kind="ExternalOutput").ap()
        dbg_ms = nc.dram_tensor("dbg_ms", [1, 1024], F32, kind="ExternalOutput").ap()
        dbg_sq = nc.dram_tensor("dbg_sq", [128, 1024], BF16, kind="ExternalOutput").ap()
        dbg_lk = nc.dram_tensor("dbg_lk", [128, 2048], BF16, kind="ExternalOutput").ap()
        dbg_sel = nc.dram_tensor("dbg_sel", [2, 32], BF16, kind="ExternalOutput").ap()
        dbg_lam = nc.dram_tensor("dbg_lam", [1, 4], F32, kind="ExternalOutput").ap()
        dbg_cvf = nc.dram_tensor("dbg_cvf", [1, len(CONSTS_F32)], F32, kind="ExternalOutput").ap()
        dbg_r = nc.dram_tensor("dbg_r", [128, 1024], BF16, kind="ExternalOutput").ap()
        dbg_ln2 = nc.dram_tensor("dbg_ln2", [128, NCH * TPC], BF16, kind="ExternalOutput").ap()

    def dbg_dump(name, dst, srcf, waits=()):
        if not KDBG:
            return
        P.op("gpsimd", lambda e: e.dma_start(dst, srcf()), waits=list(waits), inc=sG, inck=16)
        g_w = sG.n
        P.op("gpsimd", lambda e: e.nop(), waits=[(sG, g_w)])
        P.flush(nc, "dbg_" + name)

    x_sb = stk.enter_context(nc.sbuf_tensor("x_sb", [128, NCH * TPC], F32))
    AB = stk.enter_context(nc.sbuf_tensor("AB", [128, NCH * TPC], BF16))
    QT = stk.enter_context(nc.sbuf_tensor("QT", [128, NCH * TPC], BF16))
    KT = stk.enter_context(nc.sbuf_tensor("KT", [128, NCH * TPC], BF16))
    bo_sb = stk.enter_context(nc.sbuf_tensor("bo_sb", [1, DIM], BF16))
    lqk_sb = stk.enter_context(nc.sbuf_tensor("lqk_sb", [1, 256], F32))
    b2a = stk.enter_context(nc.sbuf_tensor("b2a_s", [32, 2], BF16))
    sel2b = stk.enter_context(nc.sbuf_tensor("sel2b_s", [2, 32], BF16))
    sel2f = stk.enter_context(nc.sbuf_tensor("sel2f_s", [2, 32], F32))
    sel2l = stk.enter_context(nc.sbuf_tensor("sel2l_s", [2, 32], BF16))
    o128b = stk.enter_context(nc.sbuf_tensor("o128b_s", [128, 1], BF16))
    o128f = stk.enter_context(nc.sbuf_tensor("o128f_s", [1, 128], F32))
    o128br = stk.enter_context(nc.sbuf_tensor("o128br_s", [1, 128], BF16))
    o2f = stk.enter_context(nc.sbuf_tensor("o2f_s", [1, 2], F32))
    oNb = stk.enter_context(nc.sbuf_tensor("oNb_s", [1, TPC], BF16))
    lam_t = stk.enter_context(nc.sbuf_tensor("lam_t", [1, 64], F32))
    lam12 = stk.enter_context(nc.sbuf_tensor("lam12", [1, 2], F32))
    lam_e = stk.enter_context(nc.sbuf_tensor("lam_e", [1, 2], F32))
    lam_s = stk.enter_context(nc.sbuf_tensor("lam_s", [1, 1], F32))
    lam2 = stk.enter_context(nc.sbuf_tensor("lam2", [2, 1], F32))
    lamdiag = stk.enter_context(nc.sbuf_tensor("lamdiag", [2, 2], BF16))
    cvf = stk.enter_context(nc.sbuf_tensor("cvf_s", [1, 128], F32))
    cvb = stk.enter_context(nc.sbuf_tensor("cvb_s", [1, 256], BF16))
    for i, v in enumerate(CONSTS_F32):
        nc.const_aps.register(F32, v, cvf[0:1, i:i + 1])
    for i, v in enumerate(CONSTS_B16):
        nc.const_aps.register(BF16, v, cvb[0:1, i:i + 1])

    sW = Sem(stk, nc, "sW")
    sW2 = Sem(stk, nc, "sW2")
    sG2 = Sem(stk, nc, "sG2")
    sG3 = Sem(stk, nc, "sG3")
    sS2 = Sem(stk, nc, "sS2")
    sWd = Sem(stk, nc, "sWd")
    sWd2 = Sem(stk, nc, "sWd2")
    sWg = Sem(stk, nc, "sWg")
    sGc = Sem(stk, nc, "sGc")
    sG = Sem(stk, nc, "sG")
    sS = Sem(stk, nc, "sS")
    sT = Sem(stk, nc, "sT")
    sV = Sem(stk, nc, "sV")
    sC = Sem(stk, nc, "sC")

    P = Prog()
    for _e in [s for s in SELFSYNC.split(",") if s]:
        P.chain[_e] = Sem(stk, nc, "ch_" + _e)

    for dst, src in ((bo_sb, bo_d), (lqk_sb, lqk_d), (b2a, b2a_d), (sel2b, sel2b_d),
                     (sel2f, sel2f_d), (o128b, o128b_d), (o128f, o128f_d),
                     (o128br, o128br_d), (o2f, o2f_d), (oNb, oNb_d),
                     (cvf, cvf_d), (cvb, cvb_d)):
        P.op("sync", lambda e, d=dst, s=src: e.dma_start(d[:, :], s), inc=sW, inck=16)
    w_consts = sW.n
    xv = xT.rearrange("(c p) t -> p c t", p=128)
    for i in range(4):
        P.op("gpsimd", lambda e, i=i: e.dma_start(
            x_sb[:, :].rearrange("p (c t) -> p c t", c=NCH)[:, 4 * i:4 * i + 4, :],
            xv[:, 4 * i:4 * i + 4, :]), inc=sG, inck=16)
    w_x = sG.n

    P.op("vector", lambda e: e.tensor_tensor(lam_t[:, :], lqk_sb[0:1, 0:64], lqk_sb[0:1, 128:192], ALU.mult),
         waits=[(sW, w_consts)], inc=sV)
    P.op("vector", lambda e: e.reduce_sum(lam12[0:1, 0:1], lam_t[:, :], axis=mybir.AxisListType.X), inc=sV)
    P.op("vector", lambda e: e.tensor_tensor(lam_t[:, :], lqk_sb[0:1, 64:128], lqk_sb[0:1, 192:256], ALU.mult), inc=sV)
    P.op("vector", lambda e: e.reduce_sum(lam12[0:1, 1:2], lam_t[:, :], axis=mybir.AxisListType.X), inc=sV)
    v_l = sV.n
    P.op("scalar", lambda e: e.activation(lam_e[:, :], lam12[:, :], AF.Exp), waits=[(sV, v_l)], inc=sC)
    c_l = sC.n
    c_lam_exp = sC.n
    P.op("sync", lambda e: e.nop(), waits=[(sW, sW.n)])
    P.op("gpsimd", lambda e: e.nop(), waits=[(sG, sG.n)])
    P.flush(nc, "init")
    if True:

        def emit_ln2(tag):
            with nc.psum_tensor("pSx" + tag, [1, TPC], F32) as pSx, \
                 nc.psum_tensor("pSxx" + tag, [1, TPC], F32) as pSxx, \
                 nc.psum_tensor("pBmu" + tag, [128, TPC], F32) as pBmu, \
                 nc.psum_tensor("pBr" + tag, [128, TPC], F32) as pBr, \
                 nc.sbuf_tensor("xb" + tag, [128, 2 * TPC], BF16) as xb, \
                 nc.sbuf_tensor("sqb" + tag, [128, 2 * TPC], BF16) as sqb, \
                 nc.sbuf_tensor("mu" + tag, [1, TPC], F32) as mu_sb, \
                 nc.sbuf_tensor("msq" + tag, [1, TPC], F32) as msq_sb, \
                 nc.sbuf_tensor("rb" + tag, [128, TPC], BF16) as rb_bf, \
                 nc.sbuf_tensor("m2" + tag, [1, TPC], F32) as m2_sb, \
                 nc.sbuf_tensor("tdbg" + tag, [128, 8 * TPC if KDBG else 2 * TPC], BF16) as tdbg:
                vdone = [0, 0]
                for c in range(NCH):
                    par = c % 2
                    xc = x_sb[:, TPC * c:TPC * (c + 1)]
                    xbp = xb[:, TPC * par:TPC * (par + 1)]
                    sqp = sqb[:, TPC * par:TPC * (par + 1)]
                    waits = [(sG, w_x)]
                    if c >= 2:
                        waits.append((sT, vdone[par]))
                    P.op("vector", lambda e, xbp=xbp, xc=xc: e.tensor_copy(xbp, xc), waits=waits, inc=sV)
                    P.op("vector", lambda e, sqp=sqp, xbp=xbp: e.tensor_tensor(sqp, xbp, xbp, ALU.mult), inc=sV)
                    v_c = sV.n
                    for hseg in range(2):
                        P.op("tensor", lambda e, xbp=xbp, hseg=hseg, c=c: e.matmul(
                            pSx[:, 512 * hseg:512 * (hseg + 1)], o128b[:, :],
                            xbp[:, 512 * hseg:512 * (hseg + 1)],
                            start=(c == 0), stop=(c == NCH - 1), skip_group_check=True),
                            waits=[(sV, v_c)], inc=sT)
                        P.op("tensor", lambda e, sqp=sqp, hseg=hseg, c=c: e.matmul(
                            pSxx[:, 512 * hseg:512 * (hseg + 1)], o128b[:, :],
                            sqp[:, 512 * hseg:512 * (hseg + 1)],
                            start=(c == 0), stop=(c == NCH - 1), skip_group_check=True), inc=sT)
                    vdone[par] = sT.n
                t_sums = sT.n
                P.op("vector", lambda e: e.tensor_single_scalar(mu_sb[:, :], pSx[:, :], 1.0 / DIM, ALU.mult),
                     waits=[(sT, t_sums)], inc=sV)
                P.op("vector", lambda e: e.tensor_single_scalar(msq_sb[:, :], pSxx[:, :], 1.0 / DIM, ALU.mult), inc=sV)
                v_mu = sV.n
                if KDBG and tag == "1":
                    P.op("gpsimd", lambda e: e.dma_start(dbg_msq, msq_sb[:, :]), waits=[(sV, v_mu)], inc=sG, inck=16)
                for hseg in range(2):
                    P.op("tensor", lambda e, hseg=hseg: e.matmul(
                        pBmu[:, 512 * hseg:512 * (hseg + 1)], o128f[:, :],
                        mu_sb[0:1, 512 * hseg:512 * (hseg + 1)], start=True, stop=True),
                        waits=[(sV, v_mu)], inc=sT)
                t_bmu = sT.n
                P.op("vector", lambda e: e.tensor_tensor(m2_sb[:, :], mu_sb[:, :], mu_sb[:, :], ALU.mult),
                     waits=[(sT, t_bmu)], inc=sV)
                P.op("vector", lambda e: e.tensor_tensor(msq_sb[:, :], msq_sb[:, :], m2_sb[:, :], ALU.subtract), inc=sV)
                P.op("vector", lambda e: e.tensor_single_scalar(msq_sb[:, :], msq_sb[:, :], EPS, ALU.add), inc=sV)
                v_var = sV.n
                if KDBG and tag == "1":
                    P.op("gpsimd", lambda e: e.dma_start(dbg_var, msq_sb[:, :]), waits=[(sV, v_var)], inc=sG, inck=16)
                    gv_ = sG.n
                P.op("scalar", lambda e: e.activation(mu_sb[:, :], msq_sb[:, :], AF.Ln),
                     waits=[(sV, v_var)] + ([(sG, gv_)] if (KDBG and tag == "1") else []), inc=sC)
                c_ln = sC.n
                if KDBG and tag == "1":
                    P.op("gpsimd", lambda e: e.dma_start(dbg_lnv, mu_sb[:, :]), waits=[(sC, c_ln)], inc=sG, inck=16)
                P.op("scalar", lambda e: e.activation(msq_sb[:, :], mu_sb[:, :], AF.Exp, scale=-0.5),
                     waits=[(sC, c_ln)] + ([(sG, sG.n)] if (KDBG and tag == "1") else []), inc=sC)
                c_r = sC.n
                if KDBG and tag == "1":
                    P.op("gpsimd", lambda e: e.dma_start(dbg_rsd, msq_sb[:, :]), waits=[(sC, c_r)], inc=sG, inck=16)
                for hseg in range(2):
                    P.op("tensor", lambda e, hseg=hseg: e.matmul(
                        pBr[:, 512 * hseg:512 * (hseg + 1)], o128f[:, :],
                        msq_sb[0:1, 512 * hseg:512 * (hseg + 1)], start=True, stop=True),
                        waits=[(sC, c_r)], inc=sT)
                t_br = sT.n
                P.op("vector", lambda e: e.tensor_copy(rb_bf[:, :], pBr[:, :]),
                     waits=[(sT, t_br)], inc=sV)
                c_rb = sC.n
                v_rb = sV.n
                if KDBG and tag == "1":
                    P.op("gpsimd", lambda e: e.dma_start(dbg_rb, rb_bf[:, :]), waits=[(sV, v_rb)], inc=sG, inck=16)
                    P.op("vector", lambda e: e.tensor_copy(xb[:, 0:TPC], pBmu[:, :]), waits=[(sV, v_rb)], inc=sV)
                    vdbg = sV.n
                    P.op("gpsimd", lambda e: e.dma_start(dbg_bmu, xb[:, 0:TPC]), waits=[(sV, vdbg)], inc=sG, inck=16)
                    gdbg = sG.n
                for c in range(NCH):
                    par = c % 2
                    xc = x_sb[:, TPC * c:TPC * (c + 1)]
                    tb = tdbg[:, TPC * (c % 8):TPC * (c % 8 + 1)] if KDBG else tdbg[:, TPC * par:TPC * (par + 1)]
                    abc = AB[:, TPC * c:TPC * (c + 1)]
                    waits_ap = [(sV, v_rb)]
                    if KDBG and tag == "1" and c == 0:
                        waits_ap.append((sG, gdbg))
                    P.op("vector", lambda e, tb=tb, xc=xc: e.tensor_tensor(tb, xc, pBmu[:, :], ALU.subtract),
                         waits=waits_ap, inc=sV)
                    P.op("vector", lambda e, abc=abc, tb=tb: e.tensor_tensor(abc, tb, rb_bf[:, :], ALU.mult),
                         inc=sV)
                v_apl = sV.n
                if KDBG and tag == "1":
                    P.op("gpsimd", lambda e: e.dma_start(dbg_tb[:, 0:8 * TPC], tdbg[:, :]), waits=[(sV, v_apl)], inc=sG, inck=16)
                P.flush(nc, "ln" + tag)

        emit_ln2("1")
        if KDBG:
            dbg_dump("ln1", dbg_ln1, lambda: AB[:, :])
            dbg_dump("x0", dbg_x0, lambda: x_sb[:, :])

        def emit_qk():
            with nc.sbuf_tensor("wqk", [128, 2 * DIM], BF16) as wbuf, \
                 nc.psum_tensor("pQ0", [128, TPC], F32) as pQ0, \
                 nc.psum_tensor("pQ1", [128, TPC], F32) as pQ1:
                pQ = (pQ0, pQ1)
                tdone = {}
                vdone = {}
                idx = 0
                for wsrc, dst in ((wqt, QT), (wkt, KT)):
                    wv = wsrc.rearrange("(c p) n -> p c n", p=128)
                    for co in range(NCH):
                        par = idx % 2
                        slot = wbuf[:, DIM * par:DIM * (par + 1)]
                        waits = []
                        if idx >= 2:
                            waits.append((sT, tdone[idx - 2]))
                        wsem = sW if par == 0 else sW2
                        P.op("sync", lambda e, slot=slot, wv=wv, co=co: e.dma_start(
                            slot.rearrange("p (c n) -> p c n", c=NCH),
                            wv[:, :, 128 * co:128 * (co + 1)]), waits=waits, inc=wsem, inck=16)
                        mwaits = [(wsem, wsem.n)]
                        if idx >= 2:
                            mwaits.append((sV, vdone[idx - 2]))
                        for ci in range(NCH):
                            for hseg in range(2):
                                P.op("tensor", lambda e, par=par, slot=slot, ci=ci, hseg=hseg, co=co, idx=idx: e.matmul(
                                    pQ[par][:, 512 * hseg:512 * (hseg + 1)],
                                    slot[:, 128 * ci:128 * (ci + 1)],
                                    AB[:, TPC * ci + 512 * hseg: TPC * ci + 512 * (hseg + 1)],
                                    start=(ci == 0), stop=(ci == NCH - 1)),
                                    waits=mwaits if ci == 0 and hseg == 0 else (), inc=sT)
                        tdone[idx] = sT.n
                        P.op("vector", lambda e, dst=dst, co=co, par=par: e.tensor_copy(
                            dst[:, TPC * co:TPC * (co + 1)], pQ[par][:, :]),
                            waits=[(sT, tdone[idx])], inc=sV)
                        vdone[idx] = sV.n
                        idx += 1
                P.flush(nc, "qk")

        emit_qk()
        if KDBG:
            dbg_dump("qt", dbg_qt, lambda: QT[:, :])
            dbg_dump("kt", dbg_kt, lambda: KT[:, :])

        def emit_v():
            with nc.sbuf_tensor("wvb", [128, 1024], BF16) as wvb, \
                 nc.sbuf_tensor("vtk", [128, 2 * DIM], BF16) as vtk, \
                 nc.psum_tensor("pV", [128, 8 * 512], F32) as pV:
                t_mark = {}
                v_evac = {}
                g_out = {}
                for q in range(4):
                    for ci in range(NCH):
                        par = ci % 2
                        slot = wvb[:, 512 * par:512 * (par + 1)]
                        waits = []
                        if (q, ci - 2) in t_mark:
                            waits.append((sT, t_mark[(q, ci - 2)]))
                        elif q > 0 and ci < 2:
                            waits.append((sT, t_mark[(q - 1, NCH - 2 + ci)]))
                        wsem = sW if par == 0 else sWg
                        weng = "sync" if par == 0 else "scalar"
                        P.op(weng, lambda e, slot=slot, ci=ci, q=q: e.dma_start(
                            slot, wvt[128 * ci:128 * (ci + 1), 512 * q:512 * (q + 1)]),
                            waits=waits, inc=wsem, inck=16)
                        mwaits = [(wsem, wsem.n)]
                        if q > 0 and ci == 0:
                            mwaits.append((sV, v_evac[q - 1]))
                        for tb in range(8):
                            P.op("tensor", lambda e, slot=slot, ci=ci, tb=tb: e.matmul(
                                pV[:, 512 * tb:512 * (tb + 1)],
                                AB[:, TPC * ci + 128 * tb:TPC * ci + 128 * (tb + 1)],
                                slot,
                                start=(ci == 0), stop=(ci == NCH - 1), skip_group_check=True),
                                waits=mwaits if tb == 0 else (), inc=sT)
                        t_mark[(q, ci)] = sT.n
                    t_q = sT.n
                    for tb in range(8):
                        waits = [(sT, t_q)]
                        if q > 0 and tb == 0:
                            waits.append((sG, g_out[q - 1]))
                        P.op("vector", lambda e, tb=tb, q=q: e.tensor_copy(
                            vtk[:, 512 * tb:512 * (tb + 1)],
                            pV[:, 512 * tb:512 * (tb + 1)]), waits=waits, inc=sV)
                    v_evac[q] = sV.n
                    P.op("gpsimd", lambda e, q=q: e.dma_start(
                        Vd.rearrange("(tb p) c -> p tb c", tb=8, p=128)[:, :, 512 * q:512 * (q + 1)],
                        vtk[:, :].rearrange("p (tb n) -> p tb n", tb=8, n=512)),
                        waits=[(sV, v_evac[q])], inc=sG, inck=16)
                    g_out[q] = sG.n
                P.flush(nc, "v")
        emit_v()
        g_vdram = sG.n
        if KDBG:
            dbg_dump("vd", dbg_vd, lambda: Vd, waits=[(sG, g_vdram)])

        def emit_att():
            with nc.psum_tensor("PA", [128, 2048], F32) as PA, \
                 nc.psum_tensor("PB", [128, 2048], F32) as PB, \
                 nc.sbuf_tensor("Rb", [128, 2048], BF16) as Rb2, \
                 nc.sbuf_tensor("LKb", [128, 2048], BF16) as LK2, \
                 nc.sbuf_tensor("E0s", [32, 1024], BF16) as E0s, \
                 nc.sbuf_tensor("E1s", [32, 1024], BF16) as E1s, \
                 nc.sbuf_tensor("Zb0s", [32, 1024], BF16) as Zb0s, \
                 nc.sbuf_tensor("Zb1s", [32, 1024], BF16) as Zb1s, \
                 nc.sbuf_tensor("Zrb", [2, 1024], BF16) as Zrb, \
                 nc.sbuf_tensor("V8b", [128, 1024], BF16) as V8b2, \
                 nc.sbuf_tensor("RBD", [128, 1024], BF16) as RBD2, \
                 nc.sbuf_tensor("OUTsb", [128, 1024], F32) as OUTsb, \
                 nc.sbuf_tensor("sqatt", [128, 1024], BF16) as sqb2, \
                 nc.sbuf_tensor("rstdb", [1, 1024], BF16) as rstdb, \
                 nc.sbuf_tensor("lnmsb", [1, 1024], BF16) as lnmsb, \
                 nc.sbuf_tensor("msb", [1, 1024], BF16) as msb, \
                 nc.sbuf_tensor("Ztl", [2, 1024], BF16) as Ztl:
                P.op("gpsimd", lambda e: e.memset(RBD2[:, :], 0.0), waits=[(sG, g_vdram)], inc=sGc)
                g_z = sGc.n
                P.op("vector", lambda e: e.memset(LK2[:, :], 0.0), waits=[(sG, g_vdram)], inc=sV)
                P.op("gpsimd", lambda e: e.tensor_tensor(lam_s[:, :], lam_e[0:1, 0:1], lam_e[0:1, 1:2], ALU.subtract),
                     waits=[(sC, c_lam_exp)], inc=sGc)
                P.op("gpsimd", lambda e: e.tensor_tensor(lam_s[:, :], lam_s[:, :], cvf[0:1, 7:8], ALU.add), inc=sGc)
                P.op("gpsimd", lambda e: e.memset(lamdiag[:, :], 0.0), inc=sGc)
                v_l2 = sGc.n
                P.op("gpsimd", lambda e: e.dma_start(lamdiag[0:1, 0:1], lam_s[:, :]), waits=[(sGc, v_l2)], inc=sG, inck=16)
                P.op("gpsimd", lambda e: e.dma_start(lamdiag[1:2, 1:2], lam_s[:, :]), inc=sG, inck=16)
                g_l2 = sG.n
                P.op("tensor", lambda e: e.matmul(PB[0:2, 1024:1056], lamdiag[:, :], sel2b[:, :], start=True, stop=True),
                     waits=[(sG, g_l2)], inc=sT)
                t_l2 = sT.n
                P.op("vector", lambda e: e.tensor_copy(sel2l[:, :], PB[0:2, 1024:1056]), waits=[(sT, t_l2)], inc=sV)
                v_sel2l = sV.n

                QTv = QT[:, :].rearrange("p (c q u s) -> p q s c u", c=NCH, q=16, u=32, s=2)
                KTv = KT[:, :].rearrange("p (c q u s) -> p q s c u", c=NCH, q=16, u=32, s=2)
                Vdv = Vd.rearrange("(q g t) (h d) -> q (t h) g d", q=16, g=8, t=8, h=16)
                Ddw = Dd.rearrange("(q u s) (h a) -> q s h u a", q=16, u=32, s=2, h=16)
                Ddr = Dd.rearrange("(q g s) (h a) -> q s h g a", q=16, g=8, s=8, h=16)
                E0v = E0s[:, :].rearrange("p (u c) -> p u c", u=32)
                E1v = E1s[:, :].rearrange("p (u c) -> p u c", u=32)
                OUTv = OUTsb[:, :].rearrange("p (g t a) -> p a g t", g=8, t=8)
                RBv = PB[0:128, 1024:2048].rearrange("p (g t a) -> p a g t", g=8, t=8)
                ABv = AB[:, :].rearrange("p (a q g t) -> p q a g t", a=16, q=16, g=8, t=8)

                hist = {}

                def front(ht):
                    par = ht % 2
                    Rb = Rb2[:, 1024 * par:1024 * (par + 1)]
                    LKb = LK2[:, :]
                    Rdst = Rb.rearrange("p (u c s) -> p s c u", u=32, c=16, s=2)
                    LKdst = LKb.rearrange("p (u e t c s) -> p e t s c u", u=32, e=2, t=2, c=8, s=2)
                    Dv = Rb[0:32, 0:512].rearrange("p (u a) -> p u a", u=32)
                    h2 = hist.get(ht - 2, {})
                    h1 = hist.get(ht - 1, {})
                    cur = {}
                    hist[ht] = cur
                    bwaits = [(sT, h2.get("t_att", 0)), (sG3, h2.get("g_d", 0)), (sG, g_vdram)]
                    lkwaits = [(sT, h1.get("t_att", 0))]
                    for s1 in range(2):
                        for tau in range(2):
                            P.op("vector", lambda e, s1=s1, tau=tau, ht=ht, Rdst=Rdst: e.tensor_copy(
                                Rdst[64 * tau:64 * tau + 64, s1, :, :], QTv[64 * s1:64 * s1 + 64, ht, tau, :, :]),
                                waits=bwaits, inc=sV)
                    for sl in range(2):
                        for eh in range(2):
                            for tau in range(2):
                                P.op("vector", lambda e, sl=sl, eh=eh, tau=tau, ht=ht, LKdst=LKdst: e.tensor_copy(
                                    LKdst[64 * tau:64 * tau + 64, eh, tau, sl, :, :],
                                    KTv[64 * sl:64 * sl + 64, ht, tau, 8 * eh:8 * eh + 8, :]),
                                    waits=lkwaits, inc=sV)
                    cur["v_build"] = sV.n
                    awaits = [(sV, cur["v_build"]), (sV, h1.get("v_zb", v_sel2l))]
                    for u in range(32):
                        P.op("tensor", lambda e, u=u, LKb=LKb, Rb=Rb: e.matmul(
                            PA[0:32, 32 * u:32 * (u + 1)], LKb[:, 64 * u:64 * u + 32],
                            Rb[:, 32 * u:32 * (u + 1)], start=True, stop=True),
                            waits=awaits if u == 0 else (), inc=sT)
                        P.op("tensor", lambda e, u=u, LKb=LKb, Rb=Rb: e.matmul(
                            PA[0:32, 1024 + 32 * u:1024 + 32 * (u + 1)], LKb[:, 64 * u + 32:64 * u + 64],
                            Rb[:, 32 * u:32 * (u + 1)], start=True, stop=True), inc=sT)
                    cur["t_att"] = sT.n
                    P.op("scalar", lambda e: e.activation(E0s[:, :], PA[0:32, 0:1024], AF.Exp, scale=SCALE),
                         waits=[(sT, cur["t_att"]), (sV, h1.get("v_d", 0)), (sT, h1.get("t_z", 0))], inc=sC)
                    P.op("scalar", lambda e: e.activation(E1s[:, :], PA[0:32, 1024:2048], AF.Exp, scale=SCALE),
                         inc=sC)
                    cur["c_exp"] = sC.n

                def mid(ht):
                    par = ht % 2
                    Rb = Rb2[:, 1024 * par:1024 * (par + 1)]
                    Dv = Rb[0:32, 0:512].rearrange("p (u a) -> p u a", u=32)
                    h1 = hist.get(ht - 1, {})
                    cur = hist[ht]
                    P.op("gpsimd", lambda e, ht=ht: e.dma_start(
                        V8b2[:, :].rearrange("p (g d) -> p g d", g=8), Vdv[ht]),
                        waits=[(sG, g_vdram), (sT, h1.get("t_out", 0))], inc=sG2, inck=16)
                    cur["g_v8"] = sG2.n
                    for hseg in range(2):
                        for srce, st_, sp_ in ((E0s, True, False), (E1s, False, True)):
                            P.op("tensor", lambda e, srce=srce, st_=st_, sp_=sp_, hseg=hseg: e.matmul(
                                PB[0:2, 512 * hseg:512 * (hseg + 1)], b2a[:, :],
                                srce[:, 512 * hseg:512 * (hseg + 1)],
                                start=st_, stop=sp_),
                                waits=[(sC, cur["c_exp"]), (sV, h1.get("v_sq", 0))] if st_ and hseg == 0 else (),
                                inc=sT)
                    cur["t_z"] = sT.n
                    P.op("scalar", lambda e: e.activation(Ztl[:, :], PB[0:2, 0:1024], AF.Ln),
                         waits=[(sT, cur["t_z"])], inc=sC)
                    cur["c_zl"] = sC.n
                    P.op("scalar", lambda e: e.activation(Zrb[:, :], Ztl[:, :], AF.Exp, scale=-1.0),
                         waits=[(sT, h1.get("t_zb", 0))], inc=sC)
                    cur["c_zr"] = sC.n
                    for lhs, coloff in ((sel2b, 0), (sel2l, 1024)):
                        for hseg in range(2):
                            P.op("tensor", lambda e, lhs=lhs, coloff=coloff, hseg=hseg: e.matmul(
                                PA[0:32, coloff + 512 * hseg:coloff + 512 * (hseg + 1)], lhs[:, :],
                                Zrb[0:2, 512 * hseg:512 * (hseg + 1)], start=True, stop=True),
                                waits=[(sC, cur["c_zr"]), (sV, v_sel2l)] if (coloff == 0 and hseg == 0) else (),
                                inc=sT)
                    cur["t_zb"] = sT.n
                    P.op("vector", lambda e: e.tensor_copy(Zb0s[:, :], PA[0:32, 0:1024]),
                         waits=[(sT, cur["t_zb"])], inc=sV)
                    P.op("vector", lambda e: e.tensor_copy(Zb1s[:, :], PA[0:32, 1024:2048]), inc=sV)
                    cur["v_zb"] = sV.n
                    P.op("vector", lambda e: e.tensor_tensor(E0s[:, :], E0s[:, :], Zb0s[:, :], ALU.mult), inc=sV)
                    P.op("vector", lambda e: e.tensor_tensor(E1s[:, :], E1s[:, :], Zb1s[:, :], ALU.mult), inc=sV)
                    P.op("vector", lambda e, Dv=Dv: e.tensor_tensor(Dv, E0v[:, :, 0:16], E1v[:, :, 16:32], ALU.subtract),
                         inc=sV)
                    cur["v_d"] = sV.n
                    for tau2 in range(2):
                        P.op("gpsimd", lambda e, tau2=tau2, ht=ht, Rb=Rb: e.dma_start(
                            Ddw[ht, tau2],
                            Rb[16 * tau2:16 * tau2 + 16, 0:512].rearrange("p (u a) -> p u a", u=32)),
                            waits=[(sV, cur["v_d"])], inc=sG3, inck=16)
                    cur["g_d"] = sG3.n
                    rsem = sS if ht % 2 == 0 else sS2
                    for tau in range(8):
                        P.op("sync", lambda e, tau=tau, ht=ht: e.dma_start(
                            RBD2[16 * tau:16 * tau + 16, :].rearrange("p (g c) -> p g c", g=8)[:, :, 16 * tau:16 * tau + 16],
                            Ddr[ht, tau]),
                            waits=[(sG3, cur["g_d"]), (sT, h1.get("t_out", 0)), (sGc, g_z)], inc=rsem, inck=16)
                    cur["s_rbd"] = (rsem, rsem.n)

                def back(ht):
                    if ht < 0:
                        return
                    cur = hist[ht]
                    h1 = hist.get(ht - 1, {})
                    owaits = [cur["s_rbd"], (sG2, cur["g_v8"]), (sC, cur["c_zl"])]
                    for g in range(8):
                        P.op("tensor", lambda e, g=g: e.matmul(
                            PB[0:128, 128 * g:128 * (g + 1)], V8b2[:, 128 * g:128 * (g + 1)],
                            RBD2[:, 128 * g:128 * (g + 1)], start=True, stop=True),
                            waits=owaits if g == 0 else (), inc=sT)
                    cur["t_out"] = sT.n
                    P.op("vector", lambda e: e.tensor_copy(OUTsb[:, :], PB[0:128, 0:1024]),
                         waits=[(sT, cur["t_out"]), (sC, cur["c_zl"])], inc=sV)
                    P.op("vector", lambda e: e.tensor_tensor(sqb2[:, :], OUTsb[:, :], OUTsb[:, :], ALU.mult),
                         inc=sV)
                    cur["v_sq"] = sV.n
                    cur["c_sq"] = sC.n
                    for hseg in range(2):
                        P.op("tensor", lambda e, hseg=hseg: e.matmul(
                            PB[0:1, 1024 + 512 * hseg:1024 + 512 * (hseg + 1)], o128b[:, :],
                            sqb2[:, 512 * hseg:512 * (hseg + 1)], start=True, stop=True),
                            waits=[(sV, cur["v_sq"]), (sV, h1.get("v_apply", 0)),
                                   (sC, h1.get("c_rs", 0))] if hseg == 0 else (), inc=sT)
                    cur["t_ms"] = sT.n
                    P.op("vector", lambda e: e.tensor_single_scalar(msb[:, :], PB[0:1, 1024:2048], 128.0 * EPS, ALU.add),
                         waits=[(sT, cur["t_ms"])], inc=sV)
                    cur["v_msb"] = sV.n
                    P.op("scalar", lambda e: e.activation(lnmsb[:, :], msb[:, :], AF.Ln),
                         waits=[(sV, cur["v_msb"])], inc=sC)
                    cur["c_ml"] = sC.n
                    P.op("scalar", lambda e: e.activation(rstdb[:, :], lnmsb[:, :], AF.Exp, scale=-0.5),
                         waits=[(sT, h1.get("t_rb", 0))], inc=sC)
                    cur["c_rs"] = sC.n
                    for hseg in range(2):
                        P.op("tensor", lambda e, hseg=hseg: e.matmul(
                            PB[0:128, 1024 + 512 * hseg:1024 + 512 * (hseg + 1)], o128br[:, :],
                            rstdb[0:1, 512 * hseg:512 * (hseg + 1)], start=True, stop=True),
                            waits=[(sC, cur["c_rs"])] if hseg == 0 else (), inc=sT)
                    cur["t_rb"] = sT.n
                    P.op("vector", lambda e, ht=ht: e.tensor_tensor(
                        ABv[:, ht], OUTv[:, :, :, :], RBv[:, :, :, :], ALU.mult),
                        waits=[(sT, cur["t_rb"])], inc=sV)
                    cur["v_apply"] = sV.n

                for ht in range(16):
                    front(ht)
                    back(ht - 1)
                    mid(ht)
                back(15)
                if KDBG2:
                    fin = hist[15]
                    P.op("gpsimd", lambda e: e.dma_start(dbg_es2[:, 0:1024], E0s[:, :]),
                         waits=[(sV, fin["v_d"])], inc=sG, inck=16)
                    P.op("gpsimd", lambda e: e.dma_start(dbg_es2[:, 1024:2048], E1s[:, :]), inc=sG, inck=16)
                    P.op("gpsimd", lambda e: e.dma_start(dbg_rb2x, Rb2[:, :]), inc=sG, inck=16)
                    P.op("gpsimd", lambda e: e.dma_start(dbg_lk2x, LK2[:, :]), inc=sG, inck=16)
                    P.op("gpsimd", lambda e: e.dma_start(dbg_zb2x[:, 0:1024], Zb0s[:, :]), inc=sG, inck=16)
                    P.op("gpsimd", lambda e: e.dma_start(dbg_zb2x[:, 1024:2048], Zb1s[:, :]), inc=sG, inck=16)
                    P.op("gpsimd", lambda e: e.dma_start(dbg_kt2, KT[:, :]), inc=sG, inck=16)
                    P.op("gpsimd", lambda e: e.dma_start(dbg_qt2, QT[:, :]), inc=sG, inck=16)
                P.flush(nc, "att")

        emit_att()
        if KDBG2:
            P.op("gpsimd", lambda e: e.dma_start(dbg_vd2, Vd), inc=sG, inck=16)
            P.op("gpsimd", lambda e: e.dma_start(dbg_lam2, lam_s[:, :]), inc=sG, inck=16)
            P.op("gpsimd", lambda e: e.dma_start(dbg_sel2, sel2l[:, :]), inc=sG, inck=16)
            P.op("gpsimd", lambda e: e.dma_start(dbg_dd2, Dd), inc=sG, inck=16)
            for nm_, sb_ in (("dbg_rbd2", "RBD2"), ("dbg_v82", "V8b2"), ("dbg_osb2", "OUTsb"), ("dbg_rst2", "rstdb")):
                pass
            P.op("gpsimd", lambda e: e.dma_start(dbg_outn2, AB[:, :]), inc=sG, inck=16)
            gw2 = sG.n
            P.op("gpsimd", lambda e: e.nop(), waits=[(sG, gw2)])
            P.flush(nc, "dbg2")
        if KDBG:
            dbg_dump("outn", dbg_outn, lambda: AB[:, :])

        def emit_wo():
            wov = wot.rearrange("(c p) n -> p c n", p=128)
            with nc.psum_tensor("pW0", [128, TPC], F32) as pW0, \
                 nc.psum_tensor("pW1", [128, TPC], F32) as pW1:
                pW = (pW0, pW1)
                tdone = {}
                vres = {}
                for co in range(NCH):
                    par = co % 2
                    slot = KT[:, DIM * par:DIM * (par + 1)]
                    waits = []
                    if co >= 2:
                        waits.append((sT, tdone[co - 2]))
                    wsem = sW if par == 0 else sW2
                    P.op("sync", lambda e, slot=slot, co=co: e.dma_start(
                        slot.rearrange("p (c n) -> p c n", c=NCH),
                        wov[:, :, 128 * co:128 * (co + 1)]), waits=waits, inc=wsem, inck=16)
                    w_here = wsem.n
                    mwaits = [(wsem, w_here)]
                    if co >= 2:
                        mwaits.append((sV, vres[co - 2]))
                    nm = 0
                    for a in range(NCH):
                        for hseg in range(2):
                            P.op("tensor", lambda e, par=par, slot=slot, a=a, hseg=hseg: e.matmul(
                                pW[par][:, 512 * hseg:512 * (hseg + 1)],
                                slot[:, 128 * a:128 * (a + 1)],
                                AB[:, TPC * a + 512 * hseg:TPC * a + 512 * (hseg + 1)],
                                start=(a == 0), stop=False),
                                waits=mwaits if nm == 0 else (), inc=sT)
                            nm += 1
                    for hseg in range(2):
                        P.op("tensor", lambda e, par=par, co=co, hseg=hseg: e.matmul(
                            pW[par][:, 512 * hseg:512 * (hseg + 1)],
                            bo_sb[0:1, 128 * co:128 * (co + 1)],
                            oNb[0:1, 512 * hseg:512 * (hseg + 1)],
                            start=False, stop=True), inc=sT)
                    tdone[co] = sT.n
                    P.op("vector", lambda e, co=co, par=par: e.tensor_tensor(
                        x_sb[:, TPC * co:TPC * (co + 1)], x_sb[:, TPC * co:TPC * (co + 1)],
                        pW[par][:, :], ALU.add), waits=[(sT, tdone[co])], inc=sV)
                    vres[co] = sV.n
                P.flush(nc, "wo")

        emit_wo()
        if KDBG:
            dbg_dump("x2", dbg_x2, lambda: x_sb[:, :])
        emit_ln2("2")
        if KDBG:
            dbg_dump("ln2", dbg_ln2, lambda: AB[:, :])

        def emit_ffn():
            wgv = wgp.rearrange("(c p) n -> p c n", p=128)
            wuv = wup.rearrange("(c p) n -> p c n", p=128)
            wdv = wdp.rearrange("(m b j p) n -> b p m j n", m=4, b=4, j=4, p=128)
            with nc.psum_tensor("PC1", [128, 2048], F32) as PC1, \
                 nc.psum_tensor("PC2", [128, 2048], F32) as PC2, \
                 nc.sbuf_tensor("Gs", [128, TPC], F32) as Gs, \
                 nc.sbuf_tensor("q4", [128, 4 * TPC], BF16) as q4, \
                 nc.sbuf_tensor("wins", [128, 4 * TPC], BF16) as wins, \
                 nc.sbuf_tensor("tbuf", [128, TPC], BF16) as tbuf:
                PC = (PC1, PC2)
                wg_slot = lambda par: KT[:, 2048 * par:2048 * (par + 1)]
                wu_slot = lambda par: KT[:, 4096 + 2048 * par:4096 + 2048 * (par + 1)]
                wd_slot = lambda par: KT[:, 8192 + 2048 * par:8192 + 2048 * (par + 1)]
                t_gu = {}
                c_gu = {}
                v_h = {}
                g_sq = {}
                v_mask = 0
                t_down_prev = 0
                v_yadd = {}
                prev_q_applied = 0
                cidx = 0
                for q in range(4):
                    v_mask_j = {}
                    for j in range(4):
                        for m in range(4):
                            gidx = 4 * q + j + 16 * m
                            s = 4 * m + j
                            par = cidx % 2
                            waits = []
                            if cidx >= 2:
                                waits.append((sT, t_gu.get(cidx - 2, 0)))
                            wsem = sW if par == 0 else sW2
                            P.op("sync", lambda e, par=par, gidx=gidx: e.dma_start(
                                wg_slot(par).rearrange("p (c n) -> p c n", c=NCH),
                                wgv[:, :, 128 * gidx:128 * (gidx + 1)]), waits=waits, inc=wsem, inck=16)
                            P.op("sync", lambda e, par=par, gidx=gidx: e.dma_start(
                                wu_slot(par).rearrange("p (c n) -> p c n", c=NCH),
                                wuv[:, :, 128 * gidx:128 * (gidx + 1)]), inc=wsem, inck=16)
                            mwaits = [(wsem, wsem.n)]
                            if cidx >= 2:
                                mwaits.append((sC, c_gu.get(cidx - 2, 0)))
                                mwaits.append((sV, v_h.get(cidx - 2, 0)))
                            if q > 0 and cidx % 16 < 2:
                                mwaits.append((sV, v_yadd.get((q - 1, 15 - (1 - cidx % 16)), 0)))
                            nm = 0
                            for ci in range(NCH):
                                for hseg in range(2):
                                    P.op("tensor", lambda e, par=par, ci=ci, hseg=hseg: e.matmul(
                                        PC[par][:, 512 * hseg:512 * (hseg + 1)],
                                        wg_slot(par)[:, 128 * ci:128 * (ci + 1)],
                                        AB[:, TPC * ci + 512 * hseg:TPC * ci + 512 * (hseg + 1)],
                                        start=(ci == 0), stop=(ci == NCH - 1)),
                                        waits=mwaits if nm == 0 else (), inc=sT)
                                    nm += 1
                            for ci in range(NCH):
                                for hseg in range(2):
                                    P.op("tensor", lambda e, par=par, ci=ci, hseg=hseg: e.matmul(
                                        PC[par][:, 1024 + 512 * hseg:1024 + 512 * (hseg + 1)],
                                        wu_slot(par)[:, 128 * ci:128 * (ci + 1)],
                                        AB[:, TPC * ci + 512 * hseg:TPC * ci + 512 * (hseg + 1)],
                                        start=(ci == 0), stop=(ci == NCH - 1)), inc=sT)
                            t_gu[cidx] = sT.n
                            P.op("scalar", lambda e, par=par: e.activation(
                                Gs[:, :], PC[par][:, 0:1024], AF.Silu),
                                waits=[(sT, t_gu[cidx]), (sV, v_h.get(cidx - 1, 0))], inc=sC)
                            c_gu[cidx] = sC.n
                            hwaits = [(sC, c_gu[cidx])]
                            if q > 0:
                                hwaits.append((sT, t_down_prev))
                            P.op("vector", lambda e, par=par, s=s: e.tensor_tensor(
                                QT[:, TPC * s:TPC * (s + 1)], Gs[:, :], PC[par][:, 1024:2048], ALU.mult),
                                waits=hwaits, inc=sV)
                            v_h[cidx] = sV.n
                            P.op("vector", lambda e, m=m, s=s: e.tensor_tensor(
                                q4[:, TPC * m:TPC * (m + 1)], QT[:, TPC * s:TPC * (s + 1)],
                                QT[:, TPC * s:TPC * (s + 1)], ALU.mult),
                                waits=[(sV, v_h[cidx])], inc=sV)
                            g_sq[(q, j, m)] = sV.n
                            cidx += 1
                        g_here = sV.n
                        qs = [q4[:, TPC * m:TPC * (m + 1)] for m in range(4)]
                        ws = [wins[:, TPC * m:TPC * (m + 1)] for m in range(4)]
                        mw = []
                        def vop(build, waits=()):
                            P.op("vector", build, waits=waits, inc=sV)
                        vop(lambda e: e.tensor_tensor(ws[1], qs[0], qs[1], ALU.is_ge), waits=mw)
                        vop(lambda e: e.tensor_copy(ws[0], ws[1]))
                        vop(lambda e: e.tensor_tensor(ws[2], qs[0], qs[2], ALU.is_ge))
                        vop(lambda e: e.tensor_tensor(ws[0], ws[0], ws[2], ALU.add))
                        vop(lambda e: e.tensor_tensor(ws[3], qs[0], qs[3], ALU.is_ge))
                        vop(lambda e: e.tensor_tensor(ws[0], ws[0], ws[3], ALU.add))
                        vop(lambda e: e.tensor_tensor(tbuf[:, :], qs[1], qs[2], ALU.is_ge))
                        vop(lambda e: e.tensor_tensor(ws[1], ws[1], tbuf[:, :], ALU.subtract))
                        vop(lambda e: e.tensor_tensor(ws[2], ws[2], tbuf[:, :], ALU.add))
                        vop(lambda e: e.tensor_tensor(tbuf[:, :], qs[1], qs[3], ALU.is_ge))
                        vop(lambda e: e.tensor_tensor(ws[1], ws[1], tbuf[:, :], ALU.subtract))
                        vop(lambda e: e.tensor_tensor(ws[3], ws[3], tbuf[:, :], ALU.add))
                        vop(lambda e: e.tensor_tensor(tbuf[:, :], qs[2], qs[3], ALU.is_ge))
                        vop(lambda e: e.tensor_tensor(ws[2], ws[2], tbuf[:, :], ALU.subtract))
                        vop(lambda e: e.tensor_tensor(ws[3], ws[3], tbuf[:, :], ALU.add))
                        vop(lambda e: e.tensor_single_scalar(ws[0], ws[0], 1.5, ALU.is_ge))
                        vop(lambda e: e.tensor_single_scalar(ws[1], ws[1], -0.6, ALU.is_le))
                        vop(lambda e: e.tensor_single_scalar(ws[2], ws[2], 0.4, ALU.is_le))
                        vop(lambda e: e.tensor_single_scalar(ws[3], ws[3], 1.4, ALU.is_le))
                        for m in range(4):
                            s = 4 * m + j
                            vop(lambda e, m=m, s=s: e.tensor_tensor(
                                QT[:, TPC * s:TPC * (s + 1)], QT[:, TPC * s:TPC * (s + 1)],
                                wins[:, TPC * m:TPC * (m + 1)], ALU.mult))
                        v_mask = sV.n
                        v_mask_j[j] = sV.n
                    tdd = {}
                    for co in range(NCH):
                        par = co % 2
                        waits = [(sT, t_gu[cidx - 1])]
                        if co >= 2:
                            waits = [(sT, tdd[co - 2])]
                        wsem = sWd if par == 0 else sWd2
                        for mm_ in range(4):
                            P.op("sync", lambda e, par=par, q=q, co=co, mm_=mm_: e.dma_start(
                                wd_slot(par).rearrange("p (m j n) -> p m j n", m=4, j=4)[:, mm_],
                                wdv[q][:, mm_, :, 128 * co:128 * (co + 1)]),
                                waits=waits if mm_ == 0 else (), inc=wsem, inck=16)
                        mwaits = [(wsem, wsem.n), (sV, v_mask_j[3])]
                        if co >= 2:
                            mwaits.append((sV, v_yadd[(q, co - 2)]))
                        nm = 0
                        for s in range(16):
                            for hseg in range(2):
                                P.op("tensor", lambda e, par=par, s=s, hseg=hseg: e.matmul(
                                    PC[par][:, 512 * hseg:512 * (hseg + 1)],
                                    wd_slot(par)[:, 128 * s:128 * (s + 1)],
                                    QT[:, TPC * s + 512 * hseg:TPC * s + 512 * (hseg + 1)],
                                    start=(s == 0), stop=(s == 15)),
                                    waits=mwaits if nm == 0 else (), inc=sT)
                                nm += 1
                        tdd[co] = sT.n
                        P.op("vector", lambda e, co=co, par=par: e.tensor_tensor(
                            x_sb[:, TPC * co:TPC * (co + 1)], x_sb[:, TPC * co:TPC * (co + 1)],
                            PC[par][:, 0:1024], ALU.add), waits=[(sT, tdd[co])], inc=sV)
                        v_yadd[(q, co)] = sV.n
                        if q == 3:
                            P.op("gpsimd", lambda e, co=co: e.dma_start(
                                y_d[128 * co:128 * (co + 1), :], x_sb[:, TPC * co:TPC * (co + 1)]),
                                waits=[(sV, v_yadd[(q, co)])], inc=sG, inck=16)
                    t_down_prev = sT.n
                g_final = sG.n
                P.op("gpsimd", lambda e: e.nop(), waits=[(sG, g_final)])
                P.flush(nc, "ffn")

        emit_ffn()

    stk.close()
    return nc


def _prep_inputs(inputs):
    permQ = np.empty(DIM, np.int64)
    for s in range(2):
        for h in range(H):
            for d in range(DH):
                permQ[(s * H + h) * DH + d] = h * 128 + 2 * d + s
    permI = np.empty(IFFN, np.int64)
    for g in range(2048):
        for m in range(4):
            permI[m * 2048 + g] = 4 * g + m
    b16 = lambda a: np.ascontiguousarray(a).astype(BD)
    f32 = lambda a: np.ascontiguousarray(a, dtype=np.float32)
    common = {
        "wqt": b16(inputs["L0_attn_Wq"][permQ, :].T),
        "wkt": b16(inputs["L0_attn_Wk"][permQ, :].T),
        "wvt": b16(inputs["L0_attn_Wv"].T),
        "wot": b16(inputs["L0_attn_Wo"].T),
        "wgp": b16(inputs["L0_ffn_w_gate"][:, permI]),
        "wup": b16(inputs["L0_ffn_w_up"][:, permI]),
        "wdp": b16(inputs["L0_ffn_w_down"][permI, :]),
        "bo": b16(inputs["L0_attn_bo"].reshape(1, DIM)),
        "lqk": f32(np.concatenate([inputs["L0_attn_lq1"], inputs["L0_attn_lq2"],
                                   inputs["L0_attn_lk1"], inputs["L0_attn_lk2"]]).reshape(1, 256)),
    }
    b2a = np.zeros((32, 2), np.float32)
    for k in range(32):
        b2a[k, k // 16] = 1.0
    sel2 = np.zeros((2, 32), np.float32)
    for r in range(32):
        sel2[r // 16, r] = 1.0
    common["b2a"] = b16(b2a)
    common["sel2b"] = b16(sel2)
    common["sel2f"] = f32(sel2)
    common["o128b"] = b16(np.ones((128, 1), np.float32))
    common["o128f"] = f32(np.ones((1, 128), np.float32))
    common["o128br"] = b16(np.full((1, 128), (1.0 - LAM_INIT) * float(np.sqrt(128.0)), np.float32))
    common["o2f"] = f32(np.ones((1, 2), np.float32))
    common["oNb"] = b16(np.ones((1, TPC), np.float32))
    common["cvf"] = f32(np.concatenate([np.array(CONSTS_F32, np.float64),
                                        np.zeros(128 - len(CONSTS_F32))]).reshape(1, -1))
    common["cvb"] = np.concatenate([np.array(CONSTS_B16, np.float32),
                                    np.zeros(256 - len(CONSTS_B16), np.float32)]).astype(BD).reshape(1, -1)
    x = np.asarray(inputs["L0_attn_x"], dtype=np.float32).reshape(8192, DIM)
    in_maps = []
    for c in range(NCORES):
        m = dict(common)
        m["xT"] = np.ascontiguousarray(x[TPC * c:TPC * (c + 1), :].T)
        in_maps.append(m)
    return in_maps


def kernel(**inputs):
    in_maps = _prep_inputs(inputs)
    nc = build_program()
    res = run_bass_kernel_spmd(nc, in_maps, core_ids=list(range(NCORES)))
    out = np.empty((8192, DIM), np.float32)
    for c in range(NCORES):
        out[TPC * c:TPC * (c + 1), :] = res.results[c]["y"].T
    return out.reshape(1, 8192, DIM)




# version 32
# speedup vs baseline: 1.0288x; 1.0194x over previous
import os
import numpy as np
import ml_dtypes
from contextlib import ExitStack

KDBG = bool(int(os.environ.get("KDBG", "0")))
KDBG2 = bool(int(os.environ.get("KDBG2", "0")))
SELFSYNC = os.environ.get("KSELF", "vector,scalar,gpsimd")

import concourse.bass as bass
import concourse.mybir as mybir
from concourse.bass_utils import run_bass_kernel_spmd

F32 = mybir.dt.float32
BF16 = mybir.dt.bfloat16
AF = mybir.ActivationFunctionType
ALU = mybir.AluOpType
BD = ml_dtypes.bfloat16

DIM = 2048
H = 16
DH = 64
NCH = 16
TPC = 1024
NCORES = 8
IFFN = 8192
LAYER_NUM = 12
LAM_INIT = 0.8 - 0.6 * float(np.exp(-0.3 * LAYER_NUM))
SCALE = DH ** -0.5
EPS = 1e-5
RMS_BIAS = float(np.log(1.0 - LAM_INIT))

CONSTS_F32 = [0.125, -1.0, -0.5, 1e-5, 1.0 / 128.0, RMS_BIAS, 1.0 / DIM, float(LAM_INIT), 128.0 * EPS,
              1.5, 0.5, 1.0, 2.0]
CONSTS_B16 = [-1.0, 1.0, 1.5, 0.5, -0.5, 0.125, 2.0, 0.0]

ENGS = ("sync", "gpsimd", "tensor", "scalar", "vector")


class Sem:
    def __init__(self, stk, nc, name):
        self.h = stk.enter_context(nc.semaphore(name))
        self.n = 0
        self.name = name


class Prog:

    def __init__(self):
        self.streams = {e: [] for e in ENGS}
        self.waited = {}
        self.chain = {}

    def op(self, eng, build, waits=(), inc=None, inck=1):
        ch = self.chain.get(eng)
        if ch is not None:
            last = getattr(self, "_last_done", {}).get(eng)
            if last is not None:
                waits = list(waits) + [last]
        wl = []
        for s, v in waits:
            key = (eng, s.name)
            if v > self.waited.get(key, -1):
                wl.append((s.h, v))
                self.waited[key] = v
        ih = None
        if inc is not None:
            ih = (inc.h, inck)
            inc.n += inck
        ch2 = None
        if ch is not None:
            if not hasattr(self, "_last_done"):
                self._last_done = {}
            if inc is not None:
                self._last_done[eng] = (inc, inc.n)
            else:
                k = 16 if inck == 16 else 1
                ch.n += k
                ch2 = (ch.h, k)
                self._last_done[eng] = (ch, ch.n)
        def thunk(e, wl=tuple(wl), build=build, ih=ih, ch2=ch2):
            for h, v in wl:
                e.wait_ge(h, v)
            ins = build(e)
            if ins is not None:
                if ih is not None:
                    ins.then_inc(ih[0], ih[1])
                if ch2 is not None:
                    ins.then_inc(ch2[0], ch2[1])
        self.streams[eng].append(thunk)
        return inc.n if inc is not None else None

    def flush(self, nc, name):
        streams = self.streams
        self.streams = {e: [] for e in ENGS}
        with nc.Block(name) as blk:
            if streams["sync"]:
                blk.sync(lambda e, s=streams["sync"]: [f(e) for f in s])
            if streams["gpsimd"]:
                blk.gpsimd(lambda e, s=streams["gpsimd"]: [f(e) for f in s])
            if streams["tensor"]:
                blk.tensor(lambda e, s=streams["tensor"]: [f(e) for f in s])
            if streams["scalar"]:
                blk.scalar(lambda e, s=streams["scalar"]: [f(e) for f in s])
            if streams["vector"]:
                blk.vector(lambda e, s=streams["vector"]: [f(e) for f in s])


def build_program():
    nc = bass.Bass("TRN2", target_bir_lowering=False)
    stk = ExitStack()

    xT = nc.dram_tensor("xT", [DIM, TPC], F32, kind="ExternalInput").ap()
    wqt = nc.dram_tensor("wqt", [DIM, DIM], BF16, kind="ExternalInput").ap()
    wkt = nc.dram_tensor("wkt", [DIM, DIM], BF16, kind="ExternalInput").ap()
    wvt = nc.dram_tensor("wvt", [DIM, DIM], BF16, kind="ExternalInput").ap()
    wot = nc.dram_tensor("wot", [DIM, DIM], BF16, kind="ExternalInput").ap()
    wgp = nc.dram_tensor("wgp", [DIM, IFFN], BF16, kind="ExternalInput").ap()
    wup = nc.dram_tensor("wup", [DIM, IFFN], BF16, kind="ExternalInput").ap()
    wdp = nc.dram_tensor("wdp", [IFFN, DIM], BF16, kind="ExternalInput").ap()
    bo_d = nc.dram_tensor("bo", [1, DIM], BF16, kind="ExternalInput").ap()
    lqk_d = nc.dram_tensor("lqk", [1, 256], F32, kind="ExternalInput").ap()
    b2a_d = nc.dram_tensor("b2a", [32, 2], BF16, kind="ExternalInput").ap()
    sel2b_d = nc.dram_tensor("sel2b", [2, 32], BF16, kind="ExternalInput").ap()
    sel2f_d = nc.dram_tensor("sel2f", [2, 32], F32, kind="ExternalInput").ap()
    o128b_d = nc.dram_tensor("o128b", [128, 1], BF16, kind="ExternalInput").ap()
    o128f_d = nc.dram_tensor("o128f", [1, 128], F32, kind="ExternalInput").ap()
    o128br_d = nc.dram_tensor("o128br", [1, 128], BF16, kind="ExternalInput").ap()
    o2f_d = nc.dram_tensor("o2f", [1, 2], F32, kind="ExternalInput").ap()
    oNb_d = nc.dram_tensor("oNb", [1, TPC], BF16, kind="ExternalInput").ap()
    cvf_d = nc.dram_tensor("cvf", [1, 128], F32, kind="ExternalInput").ap()
    cvb_d = nc.dram_tensor("cvb", [1, 256], BF16, kind="ExternalInput").ap()
    y_d = nc.dram_tensor("y", [DIM, TPC], F32, kind="ExternalOutput").ap()
    Vd = nc.dram_tensor("Vd", [TPC, DIM], BF16, kind="Internal").ap()
    Dd = nc.dram_tensor("Dd", [TPC, 256], BF16, kind="Internal").ap()
    if KDBG2:
        dbg_outn2 = nc.dram_tensor("dbg_outn2", [128, NCH * TPC], BF16, kind="ExternalOutput").ap()
        dbg_vd2 = nc.dram_tensor("dbg_vd2", [TPC, DIM], BF16, kind="ExternalOutput").ap()
        dbg_lam2 = nc.dram_tensor("dbg_lam2", [1, 1], F32, kind="ExternalOutput").ap()
        dbg_sel2 = nc.dram_tensor("dbg_sel2", [2, 32], BF16, kind="ExternalOutput").ap()
        dbg_dd2 = nc.dram_tensor("dbg_dd2", [TPC, 256], BF16, kind="ExternalOutput").ap()
        dbg_rbd2 = nc.dram_tensor("dbg_rbd2", [128, 1024], BF16, kind="ExternalOutput").ap()
        dbg_v82 = nc.dram_tensor("dbg_v82", [128, 1024], BF16, kind="ExternalOutput").ap()
        dbg_osb2 = nc.dram_tensor("dbg_osb2", [128, 1024], F32, kind="ExternalOutput").ap()
        dbg_rst2 = nc.dram_tensor("dbg_rst2", [1, 1024], BF16, kind="ExternalOutput").ap()
        dbg_es2 = nc.dram_tensor("dbg_es2", [32, 2048], BF16, kind="ExternalOutput").ap()
        dbg_rb2x = nc.dram_tensor("dbg_rb2x", [128, 2048], BF16, kind="ExternalOutput").ap()
        dbg_lk2x = nc.dram_tensor("dbg_lk2x", [128, 2048], BF16, kind="ExternalOutput").ap()
        dbg_zb2x = nc.dram_tensor("dbg_zb2x", [32, 2048], BF16, kind="ExternalOutput").ap()
        dbg_kt2 = nc.dram_tensor("dbg_kt2", [128, NCH * TPC], BF16, kind="ExternalOutput").ap()
        dbg_qt2 = nc.dram_tensor("dbg_qt2", [128, NCH * TPC], BF16, kind="ExternalOutput").ap()
    if KDBG:
        dbg_ln1 = nc.dram_tensor("dbg_ln1", [128, NCH * TPC], BF16, kind="ExternalOutput").ap()
        dbg_qt = nc.dram_tensor("dbg_qt", [128, NCH * TPC], BF16, kind="ExternalOutput").ap()
        dbg_kt = nc.dram_tensor("dbg_kt", [128, NCH * TPC], BF16, kind="ExternalOutput").ap()
        dbg_vd = nc.dram_tensor("dbg_vd", [TPC, DIM], BF16, kind="ExternalOutput").ap()
        dbg_outn = nc.dram_tensor("dbg_outn", [128, NCH * TPC], BF16, kind="ExternalOutput").ap()
        dbg_x2 = nc.dram_tensor("dbg_x2", [128, NCH * TPC], F32, kind="ExternalOutput").ap()
        dbg_e0 = nc.dram_tensor("dbg_e0", [32, 1024], BF16, kind="ExternalOutput").ap()
        dbg_e1 = nc.dram_tensor("dbg_e1", [32, 1024], BF16, kind="ExternalOutput").ap()
        dbg_zr = nc.dram_tensor("dbg_zr", [2, 1024], BF16, kind="ExternalOutput").ap()
        dbg_zb0 = nc.dram_tensor("dbg_zb0", [32, 1024], BF16, kind="ExternalOutput").ap()
        dbg_zb1 = nc.dram_tensor("dbg_zb1", [32, 1024], BF16, kind="ExternalOutput").ap()
        dbg_dm = nc.dram_tensor("dbg_dm", [32, 512], BF16, kind="ExternalOutput").ap()
        dbg_rbd = nc.dram_tensor("dbg_rbd", [128, 1024], BF16, kind="ExternalOutput").ap()
        dbg_v8 = nc.dram_tensor("dbg_v8", [128, 1024], BF16, kind="ExternalOutput").ap()
        dbg_osb = nc.dram_tensor("dbg_osb", [128, 1024], F32, kind="ExternalOutput").ap()
        dbg_rst = nc.dram_tensor("dbg_rst", [1, 1024], BF16, kind="ExternalOutput").ap()
        dbg_lnms = nc.dram_tensor("dbg_lnms", [1, 1024], F32, kind="ExternalOutput").ap()
        dbg_x0 = nc.dram_tensor("dbg_x0", [128, NCH * TPC], F32, kind="ExternalOutput").ap()
        dbg_rb = nc.dram_tensor("dbg_rb", [128, TPC], BF16, kind="ExternalOutput").ap()
        dbg_bmu = nc.dram_tensor("dbg_bmu", [128, TPC], F32, kind="ExternalOutput").ap()
        dbg_msq = nc.dram_tensor("dbg_msq", [1, TPC], F32, kind="ExternalOutput").ap()
        dbg_var = nc.dram_tensor("dbg_var", [1, TPC], F32, kind="ExternalOutput").ap()
        dbg_lnv = nc.dram_tensor("dbg_lnv", [1, TPC], F32, kind="ExternalOutput").ap()
        dbg_rsd = nc.dram_tensor("dbg_rsd", [1, TPC], F32, kind="ExternalOutput").ap()
        dbg_tb = nc.dram_tensor("dbg_tb", [128, NCH * TPC], BF16, kind="ExternalOutput").ap()
        dbg_ms = nc.dram_tensor("dbg_ms", [1, 1024], F32, kind="ExternalOutput").ap()
        dbg_sq = nc.dram_tensor("dbg_sq", [128, 1024], BF16, kind="ExternalOutput").ap()
        dbg_lk = nc.dram_tensor("dbg_lk", [128, 2048], BF16, kind="ExternalOutput").ap()
        dbg_sel = nc.dram_tensor("dbg_sel", [2, 32], BF16, kind="ExternalOutput").ap()
        dbg_lam = nc.dram_tensor("dbg_lam", [1, 4], F32, kind="ExternalOutput").ap()
        dbg_cvf = nc.dram_tensor("dbg_cvf", [1, len(CONSTS_F32)], F32, kind="ExternalOutput").ap()
        dbg_r = nc.dram_tensor("dbg_r", [128, 1024], BF16, kind="ExternalOutput").ap()
        dbg_ln2 = nc.dram_tensor("dbg_ln2", [128, NCH * TPC], BF16, kind="ExternalOutput").ap()

    def dbg_dump(name, dst, srcf, waits=()):
        if not KDBG:
            return
        P.op("gpsimd", lambda e: e.dma_start(dst, srcf()), waits=list(waits), inc=sG, inck=16)
        g_w = sG.n
        P.op("gpsimd", lambda e: e.nop(), waits=[(sG, g_w)])
        P.flush(nc, "dbg_" + name)

    x_sb = stk.enter_context(nc.sbuf_tensor("x_sb", [128, NCH * TPC], F32))
    AB = stk.enter_context(nc.sbuf_tensor("AB", [128, NCH * TPC], BF16))
    QT = stk.enter_context(nc.sbuf_tensor("QT", [128, NCH * TPC], BF16))
    KT = stk.enter_context(nc.sbuf_tensor("KT", [128, NCH * TPC], BF16))
    bo_sb = stk.enter_context(nc.sbuf_tensor("bo_sb", [1, DIM], BF16))
    lqk_sb = stk.enter_context(nc.sbuf_tensor("lqk_sb", [1, 256], F32))
    b2a = stk.enter_context(nc.sbuf_tensor("b2a_s", [32, 2], BF16))
    sel2b = stk.enter_context(nc.sbuf_tensor("sel2b_s", [2, 32], BF16))
    sel2f = stk.enter_context(nc.sbuf_tensor("sel2f_s", [2, 32], F32))
    sel2l = stk.enter_context(nc.sbuf_tensor("sel2l_s", [2, 32], BF16))
    o128b = stk.enter_context(nc.sbuf_tensor("o128b_s", [128, 1], BF16))
    o128f = stk.enter_context(nc.sbuf_tensor("o128f_s", [1, 128], F32))
    o128br = stk.enter_context(nc.sbuf_tensor("o128br_s", [1, 128], BF16))
    o2f = stk.enter_context(nc.sbuf_tensor("o2f_s", [1, 2], F32))
    oNb = stk.enter_context(nc.sbuf_tensor("oNb_s", [1, TPC], BF16))
    lam_t = stk.enter_context(nc.sbuf_tensor("lam_t", [1, 64], F32))
    lam12 = stk.enter_context(nc.sbuf_tensor("lam12", [1, 2], F32))
    lam_e = stk.enter_context(nc.sbuf_tensor("lam_e", [1, 2], F32))
    lam_s = stk.enter_context(nc.sbuf_tensor("lam_s", [1, 1], F32))
    lam2 = stk.enter_context(nc.sbuf_tensor("lam2", [2, 1], F32))
    lamdiag = stk.enter_context(nc.sbuf_tensor("lamdiag", [2, 2], BF16))
    cvf = stk.enter_context(nc.sbuf_tensor("cvf_s", [1, 128], F32))
    cvb = stk.enter_context(nc.sbuf_tensor("cvb_s", [1, 256], BF16))
    for i, v in enumerate(CONSTS_F32):
        nc.const_aps.register(F32, v, cvf[0:1, i:i + 1])
    for i, v in enumerate(CONSTS_B16):
        nc.const_aps.register(BF16, v, cvb[0:1, i:i + 1])

    sW = Sem(stk, nc, "sW")
    sW2 = Sem(stk, nc, "sW2")
    sG2 = Sem(stk, nc, "sG2")
    sG3 = Sem(stk, nc, "sG3")
    sS2 = Sem(stk, nc, "sS2")
    sWd = Sem(stk, nc, "sWd")
    sWd2 = Sem(stk, nc, "sWd2")
    sWg = Sem(stk, nc, "sWg")
    sWg2 = Sem(stk, nc, "sWg2")
    sGc = Sem(stk, nc, "sGc")
    sG = Sem(stk, nc, "sG")
    sS = Sem(stk, nc, "sS")
    sT = Sem(stk, nc, "sT")
    sV = Sem(stk, nc, "sV")
    sC = Sem(stk, nc, "sC")

    P = Prog()
    for _e in [s for s in SELFSYNC.split(",") if s]:
        P.chain[_e] = Sem(stk, nc, "ch_" + _e)

    for dst, src in ((bo_sb, bo_d), (lqk_sb, lqk_d), (b2a, b2a_d), (sel2b, sel2b_d),
                     (sel2f, sel2f_d), (o128b, o128b_d), (o128f, o128f_d),
                     (o128br, o128br_d), (o2f, o2f_d), (oNb, oNb_d),
                     (cvf, cvf_d), (cvb, cvb_d)):
        P.op("sync", lambda e, d=dst, s=src: e.dma_start(d[:, :], s), inc=sW, inck=16)
    w_consts = sW.n
    xv = xT.rearrange("(c p) t -> p c t", p=128)
    for i in range(4):
        P.op("gpsimd", lambda e, i=i: e.dma_start(
            x_sb[:, :].rearrange("p (c t) -> p c t", c=NCH)[:, 4 * i:4 * i + 4, :],
            xv[:, 4 * i:4 * i + 4, :]), inc=sG, inck=16)
    w_x = sG.n

    P.op("vector", lambda e: e.tensor_tensor(lam_t[:, :], lqk_sb[0:1, 0:64], lqk_sb[0:1, 128:192], ALU.mult),
         waits=[(sW, w_consts)], inc=sV)
    P.op("vector", lambda e: e.reduce_sum(lam12[0:1, 0:1], lam_t[:, :], axis=mybir.AxisListType.X), inc=sV)
    P.op("vector", lambda e: e.tensor_tensor(lam_t[:, :], lqk_sb[0:1, 64:128], lqk_sb[0:1, 192:256], ALU.mult), inc=sV)
    P.op("vector", lambda e: e.reduce_sum(lam12[0:1, 1:2], lam_t[:, :], axis=mybir.AxisListType.X), inc=sV)
    v_l = sV.n
    P.op("scalar", lambda e: e.activation(lam_e[:, :], lam12[:, :], AF.Exp), waits=[(sV, v_l)], inc=sC)
    c_l = sC.n
    c_lam_exp = sC.n
    P.op("sync", lambda e: e.nop(), waits=[(sW, sW.n)])
    P.op("gpsimd", lambda e: e.nop(), waits=[(sG, sG.n)])
    P.flush(nc, "init")
    if True:

        def emit_ln2(tag):
            with nc.psum_tensor("pSx" + tag, [1, TPC], F32) as pSx, \
                 nc.psum_tensor("pSxx" + tag, [1, TPC], F32) as pSxx, \
                 nc.psum_tensor("pBmu" + tag, [128, TPC], F32) as pBmu, \
                 nc.psum_tensor("pBr" + tag, [128, TPC], F32) as pBr, \
                 nc.sbuf_tensor("xb" + tag, [128, 2 * TPC], BF16) as xb, \
                 nc.sbuf_tensor("sqb" + tag, [128, 2 * TPC], BF16) as sqb, \
                 nc.sbuf_tensor("mu" + tag, [1, TPC], F32) as mu_sb, \
                 nc.sbuf_tensor("msq" + tag, [1, TPC], F32) as msq_sb, \
                 nc.sbuf_tensor("rb" + tag, [128, TPC], BF16) as rb_bf, \
                 nc.sbuf_tensor("m2" + tag, [1, TPC], F32) as m2_sb, \
                 nc.sbuf_tensor("tdbg" + tag, [128, 8 * TPC if KDBG else 2 * TPC], BF16) as tdbg:
                vdone = [0, 0]
                for c in range(NCH):
                    par = c % 2
                    xc = x_sb[:, TPC * c:TPC * (c + 1)]
                    xbp = xb[:, TPC * par:TPC * (par + 1)]
                    sqp = sqb[:, TPC * par:TPC * (par + 1)]
                    waits = [(sG, w_x)]
                    if c >= 2:
                        waits.append((sT, vdone[par]))
                    P.op("vector", lambda e, xbp=xbp, xc=xc: e.tensor_copy(xbp, xc), waits=waits, inc=sV)
                    P.op("vector", lambda e, sqp=sqp, xbp=xbp: e.tensor_tensor(sqp, xbp, xbp, ALU.mult), inc=sV)
                    v_c = sV.n
                    for hseg in range(2):
                        P.op("tensor", lambda e, xbp=xbp, hseg=hseg, c=c: e.matmul(
                            pSx[:, 512 * hseg:512 * (hseg + 1)], o128b[:, :],
                            xbp[:, 512 * hseg:512 * (hseg + 1)],
                            start=(c == 0), stop=(c == NCH - 1), skip_group_check=True),
                            waits=[(sV, v_c)], inc=sT)
                        P.op("tensor", lambda e, sqp=sqp, hseg=hseg, c=c: e.matmul(
                            pSxx[:, 512 * hseg:512 * (hseg + 1)], o128b[:, :],
                            sqp[:, 512 * hseg:512 * (hseg + 1)],
                            start=(c == 0), stop=(c == NCH - 1), skip_group_check=True), inc=sT)
                    vdone[par] = sT.n
                t_sums = sT.n
                P.op("vector", lambda e: e.tensor_single_scalar(mu_sb[:, :], pSx[:, :], 1.0 / DIM, ALU.mult),
                     waits=[(sT, t_sums)], inc=sV)
                P.op("vector", lambda e: e.tensor_single_scalar(msq_sb[:, :], pSxx[:, :], 1.0 / DIM, ALU.mult), inc=sV)
                v_mu = sV.n
                if KDBG and tag == "1":
                    P.op("gpsimd", lambda e: e.dma_start(dbg_msq, msq_sb[:, :]), waits=[(sV, v_mu)], inc=sG, inck=16)
                for hseg in range(2):
                    P.op("tensor", lambda e, hseg=hseg: e.matmul(
                        pBmu[:, 512 * hseg:512 * (hseg + 1)], o128f[:, :],
                        mu_sb[0:1, 512 * hseg:512 * (hseg + 1)], start=True, stop=True),
                        waits=[(sV, v_mu)], inc=sT)
                t_bmu = sT.n
                P.op("vector", lambda e: e.tensor_tensor(m2_sb[:, :], mu_sb[:, :], mu_sb[:, :], ALU.mult),
                     waits=[(sT, t_bmu)], inc=sV)
                P.op("vector", lambda e: e.tensor_tensor(msq_sb[:, :], msq_sb[:, :], m2_sb[:, :], ALU.subtract), inc=sV)
                P.op("vector", lambda e: e.tensor_single_scalar(msq_sb[:, :], msq_sb[:, :], EPS, ALU.add), inc=sV)
                v_var = sV.n
                if KDBG and tag == "1":
                    P.op("gpsimd", lambda e: e.dma_start(dbg_var, msq_sb[:, :]), waits=[(sV, v_var)], inc=sG, inck=16)
                    gv_ = sG.n
                P.op("scalar", lambda e: e.activation(mu_sb[:, :], msq_sb[:, :], AF.Ln),
                     waits=[(sV, v_var)] + ([(sG, gv_)] if (KDBG and tag == "1") else []), inc=sC)
                c_ln = sC.n
                if KDBG and tag == "1":
                    P.op("gpsimd", lambda e: e.dma_start(dbg_lnv, mu_sb[:, :]), waits=[(sC, c_ln)], inc=sG, inck=16)
                P.op("scalar", lambda e: e.activation(msq_sb[:, :], mu_sb[:, :], AF.Exp, scale=-0.5),
                     waits=[(sC, c_ln)] + ([(sG, sG.n)] if (KDBG and tag == "1") else []), inc=sC)
                c_r = sC.n
                if KDBG and tag == "1":
                    P.op("gpsimd", lambda e: e.dma_start(dbg_rsd, msq_sb[:, :]), waits=[(sC, c_r)], inc=sG, inck=16)
                for hseg in range(2):
                    P.op("tensor", lambda e, hseg=hseg: e.matmul(
                        pBr[:, 512 * hseg:512 * (hseg + 1)], o128f[:, :],
                        msq_sb[0:1, 512 * hseg:512 * (hseg + 1)], start=True, stop=True),
                        waits=[(sC, c_r)], inc=sT)
                t_br = sT.n
                P.op("vector", lambda e: e.tensor_copy(rb_bf[:, :], pBr[:, :]),
                     waits=[(sT, t_br)], inc=sV)
                c_rb = sC.n
                v_rb = sV.n
                if KDBG and tag == "1":
                    P.op("gpsimd", lambda e: e.dma_start(dbg_rb, rb_bf[:, :]), waits=[(sV, v_rb)], inc=sG, inck=16)
                    P.op("vector", lambda e: e.tensor_copy(xb[:, 0:TPC], pBmu[:, :]), waits=[(sV, v_rb)], inc=sV)
                    vdbg = sV.n
                    P.op("gpsimd", lambda e: e.dma_start(dbg_bmu, xb[:, 0:TPC]), waits=[(sV, vdbg)], inc=sG, inck=16)
                    gdbg = sG.n
                for c in range(NCH):
                    par = c % 2
                    xc = x_sb[:, TPC * c:TPC * (c + 1)]
                    tb = tdbg[:, TPC * (c % 8):TPC * (c % 8 + 1)] if KDBG else tdbg[:, TPC * par:TPC * (par + 1)]
                    abc = AB[:, TPC * c:TPC * (c + 1)]
                    waits_ap = [(sV, v_rb)]
                    if KDBG and tag == "1" and c == 0:
                        waits_ap.append((sG, gdbg))
                    P.op("vector", lambda e, tb=tb, xc=xc: e.tensor_tensor(tb, xc, pBmu[:, :], ALU.subtract),
                         waits=waits_ap, inc=sV)
                    P.op("vector", lambda e, abc=abc, tb=tb: e.tensor_tensor(abc, tb, rb_bf[:, :], ALU.mult),
                         inc=sV)
                v_apl = sV.n
                if KDBG and tag == "1":
                    P.op("gpsimd", lambda e: e.dma_start(dbg_tb[:, 0:8 * TPC], tdbg[:, :]), waits=[(sV, v_apl)], inc=sG, inck=16)
                P.flush(nc, "ln" + tag)

        emit_ln2("1")
        if KDBG:
            dbg_dump("ln1", dbg_ln1, lambda: AB[:, :])
            dbg_dump("x0", dbg_x0, lambda: x_sb[:, :])

        def emit_qk():
            with nc.sbuf_tensor("wqk", [128, 2 * DIM], BF16) as wbuf, \
                 nc.psum_tensor("pQ0", [128, TPC], F32) as pQ0, \
                 nc.psum_tensor("pQ1", [128, TPC], F32) as pQ1:
                pQ = (pQ0, pQ1)
                tdone = {}
                vdone = {}
                idx = 0
                for wsrc, dst in ((wqt, QT), (wkt, KT)):
                    wv = wsrc.rearrange("(c p) n -> p c n", p=128)
                    for co in range(NCH):
                        par = idx % 2
                        slot = wbuf[:, DIM * par:DIM * (par + 1)]
                        waits = []
                        if idx >= 2:
                            waits.append((sT, tdone[idx - 2]))
                        wsem = sW if par == 0 else sW2
                        P.op("sync", lambda e, slot=slot, wv=wv, co=co: e.dma_start(
                            slot.rearrange("p (c n) -> p c n", c=NCH),
                            wv[:, :, 128 * co:128 * (co + 1)]), waits=waits, inc=wsem, inck=16)
                        mwaits = [(wsem, wsem.n)]
                        if idx >= 2:
                            mwaits.append((sV, vdone[idx - 2]))
                        for ci in range(NCH):
                            for hseg in range(2):
                                P.op("tensor", lambda e, par=par, slot=slot, ci=ci, hseg=hseg, co=co, idx=idx: e.matmul(
                                    pQ[par][:, 512 * hseg:512 * (hseg + 1)],
                                    slot[:, 128 * ci:128 * (ci + 1)],
                                    AB[:, TPC * ci + 512 * hseg: TPC * ci + 512 * (hseg + 1)],
                                    start=(ci == 0), stop=(ci == NCH - 1)),
                                    waits=mwaits if ci == 0 and hseg == 0 else (), inc=sT)
                        tdone[idx] = sT.n
                        P.op("vector", lambda e, dst=dst, co=co, par=par: e.tensor_copy(
                            dst[:, TPC * co:TPC * (co + 1)], pQ[par][:, :]),
                            waits=[(sT, tdone[idx])], inc=sV)
                        vdone[idx] = sV.n
                        idx += 1
                P.flush(nc, "qk")

        emit_qk()
        if KDBG:
            dbg_dump("qt", dbg_qt, lambda: QT[:, :])
            dbg_dump("kt", dbg_kt, lambda: KT[:, :])

        def emit_v():
            with nc.sbuf_tensor("wvb", [128, 2048], BF16) as wvb, \
                 nc.sbuf_tensor("vtk", [128, 2 * DIM], BF16) as vtk, \
                 nc.psum_tensor("pV", [128, 8 * 512], F32) as pV:
                t_mark = {}
                v_evac = {}
                g_out = {}
                for q in range(4):
                    for ci in range(NCH):
                        par = ci % 4
                        slot = wvb[:, 512 * par:512 * (par + 1)]
                        waits = []
                        if (q, ci - 4) in t_mark:
                            waits.append((sT, t_mark[(q, ci - 4)]))
                        elif q > 0 and ci < 4:
                            waits.append((sT, t_mark[(q - 1, NCH - 4 + ci)]))
                        wsem = (sW, sW2, sWg, sWg2)[par]
                        P.op("sync", lambda e, slot=slot, ci=ci, q=q: e.dma_start(
                            slot, wvt[128 * ci:128 * (ci + 1), 512 * q:512 * (q + 1)]),
                            waits=waits, inc=wsem, inck=16)
                        mwaits = [(wsem, wsem.n)]
                        if q > 0 and ci == 0:
                            mwaits.append((sV, v_evac[q - 1]))
                        for tb in range(8):
                            P.op("tensor", lambda e, slot=slot, ci=ci, tb=tb: e.matmul(
                                pV[:, 512 * tb:512 * (tb + 1)],
                                AB[:, TPC * ci + 128 * tb:TPC * ci + 128 * (tb + 1)],
                                slot,
                                start=(ci == 0), stop=(ci == NCH - 1), skip_group_check=True),
                                waits=mwaits if tb == 0 else (), inc=sT)
                        t_mark[(q, ci)] = sT.n
                    t_q = sT.n
                    for tb in range(8):
                        waits = [(sT, t_q)]
                        if q > 0 and tb == 0:
                            waits.append((sG, g_out[q - 1]))
                        P.op("vector", lambda e, tb=tb, q=q: e.tensor_copy(
                            vtk[:, 512 * tb:512 * (tb + 1)],
                            pV[:, 512 * tb:512 * (tb + 1)]), waits=waits, inc=sV)
                    v_evac[q] = sV.n
                    P.op("gpsimd", lambda e, q=q: e.dma_start(
                        Vd.rearrange("(tb p) c -> p tb c", tb=8, p=128)[:, :, 512 * q:512 * (q + 1)],
                        vtk[:, :].rearrange("p (tb n) -> p tb n", tb=8, n=512)),
                        waits=[(sV, v_evac[q])], inc=sG, inck=16)
                    g_out[q] = sG.n
                P.flush(nc, "v")
        emit_v()
        g_vdram = sG.n
        if KDBG:
            dbg_dump("vd", dbg_vd, lambda: Vd, waits=[(sG, g_vdram)])

        def emit_att():
            with nc.psum_tensor("PA", [128, 2048], F32) as PA, \
                 nc.psum_tensor("PB", [128, 2048], F32) as PB, \
                 nc.sbuf_tensor("Rb", [128, 2048], BF16) as Rb2, \
                 nc.sbuf_tensor("LKb", [128, 2048], BF16) as LK2, \
                 nc.sbuf_tensor("E0s", [32, 1024], BF16) as E0s, \
                 nc.sbuf_tensor("E1s", [32, 1024], BF16) as E1s, \
                 nc.sbuf_tensor("Zb0s", [32, 1024], BF16) as Zb0s, \
                 nc.sbuf_tensor("Zb1s", [32, 1024], BF16) as Zb1s, \
                 nc.sbuf_tensor("Zrb", [2, 1024], BF16) as Zrb, \
                 nc.sbuf_tensor("V8b", [128, 1024], BF16) as V8b2, \
                 nc.sbuf_tensor("RBD", [128, 1024], BF16) as RBD2, \
                 nc.sbuf_tensor("OUTsb", [128, 1024], F32) as OUTsb, \
                 nc.sbuf_tensor("sqatt", [128, 1024], BF16) as sqb2, \
                 nc.sbuf_tensor("rstdb", [1, 1024], BF16) as rstdb, \
                 nc.sbuf_tensor("lnmsb", [1, 1024], BF16) as lnmsb, \
                 nc.sbuf_tensor("msb", [1, 1024], BF16) as msb, \
                 nc.sbuf_tensor("Ztl", [2, 1024], BF16) as Ztl:
                P.op("gpsimd", lambda e: e.memset(RBD2[:, :], 0.0), waits=[(sG, g_vdram)], inc=sGc)
                g_z = sGc.n
                P.op("vector", lambda e: e.memset(LK2[:, :], 0.0), waits=[(sG, g_vdram)], inc=sV)
                P.op("gpsimd", lambda e: e.tensor_tensor(lam_s[:, :], lam_e[0:1, 0:1], lam_e[0:1, 1:2], ALU.subtract),
                     waits=[(sC, c_lam_exp)], inc=sGc)
                P.op("gpsimd", lambda e: e.tensor_tensor(lam_s[:, :], lam_s[:, :], cvf[0:1, 7:8], ALU.add), inc=sGc)
                P.op("gpsimd", lambda e: e.memset(lamdiag[:, :], 0.0), inc=sGc)
                v_l2 = sGc.n
                P.op("gpsimd", lambda e: e.dma_start(lamdiag[0:1, 0:1], lam_s[:, :]), waits=[(sGc, v_l2)], inc=sG, inck=16)
                P.op("gpsimd", lambda e: e.dma_start(lamdiag[1:2, 1:2], lam_s[:, :]), inc=sG, inck=16)
                g_l2 = sG.n
                P.op("tensor", lambda e: e.matmul(PB[0:2, 1024:1056], lamdiag[:, :], sel2b[:, :], start=True, stop=True),
                     waits=[(sG, g_l2)], inc=sT)
                t_l2 = sT.n
                P.op("vector", lambda e: e.tensor_copy(sel2l[:, :], PB[0:2, 1024:1056]), waits=[(sT, t_l2)], inc=sV)
                v_sel2l = sV.n

                QTv = QT[:, :].rearrange("p (c q u s) -> p q s c u", c=NCH, q=16, u=32, s=2)
                KTv = KT[:, :].rearrange("p (c q u s) -> p q s c u", c=NCH, q=16, u=32, s=2)
                Vdv = Vd.rearrange("(q g t) (h d) -> q (t h) g d", q=16, g=8, t=8, h=16)
                Ddw = Dd.rearrange("(q u s) (h a) -> q s h u a", q=16, u=32, s=2, h=16)
                Ddr = Dd.rearrange("(q g s) (h a) -> q s h g a", q=16, g=8, s=8, h=16)
                E0v = E0s[:, :].rearrange("p (u c) -> p u c", u=32)
                E1v = E1s[:, :].rearrange("p (u c) -> p u c", u=32)
                OUTv = OUTsb[:, :].rearrange("p (g t a) -> p a g t", g=8, t=8)
                RBv = PB[0:128, 1024:2048].rearrange("p (g t a) -> p a g t", g=8, t=8)
                ABv = AB[:, :].rearrange("p (a q g t) -> p q a g t", a=16, q=16, g=8, t=8)

                hist = {}

                def front(ht):
                    par = ht % 2
                    Rb = Rb2[:, 1024 * par:1024 * (par + 1)]
                    LKb = LK2[:, :]
                    Rdst = Rb.rearrange("p (u c s) -> p s c u", u=32, c=16, s=2)
                    LKdst = LKb.rearrange("p (u e t c s) -> p e t s c u", u=32, e=2, t=2, c=8, s=2)
                    Dv = Rb[0:32, 0:512].rearrange("p (u a) -> p u a", u=32)
                    h2 = hist.get(ht - 2, {})
                    h1 = hist.get(ht - 1, {})
                    cur = {}
                    hist[ht] = cur
                    bwaits = [(sT, h2.get("t_att", 0)), (sG3, h2.get("g_d", 0)), (sG, g_vdram)]
                    lkwaits = [(sT, h1.get("t_att", 0))]
                    for s1 in range(2):
                        for tau in range(2):
                            P.op("vector", lambda e, s1=s1, tau=tau, ht=ht, Rdst=Rdst: e.tensor_copy(
                                Rdst[64 * tau:64 * tau + 64, s1, :, :], QTv[64 * s1:64 * s1 + 64, ht, tau, :, :]),
                                waits=bwaits, inc=sV)
                    for sl in range(2):
                        for eh in range(2):
                            for tau in range(2):
                                P.op("vector", lambda e, sl=sl, eh=eh, tau=tau, ht=ht, LKdst=LKdst: e.tensor_copy(
                                    LKdst[64 * tau:64 * tau + 64, eh, tau, sl, :, :],
                                    KTv[64 * sl:64 * sl + 64, ht, tau, 8 * eh:8 * eh + 8, :]),
                                    waits=lkwaits, inc=sV)
                    cur["v_build"] = sV.n
                    awaits = [(sV, cur["v_build"]), (sV, h1.get("v_zb", v_sel2l))]
                    for u in range(32):
                        P.op("tensor", lambda e, u=u, LKb=LKb, Rb=Rb: e.matmul(
                            PA[0:32, 32 * u:32 * (u + 1)], LKb[:, 64 * u:64 * u + 32],
                            Rb[:, 32 * u:32 * (u + 1)], start=True, stop=True),
                            waits=awaits if u == 0 else (), inc=sT)
                        P.op("tensor", lambda e, u=u, LKb=LKb, Rb=Rb: e.matmul(
                            PA[0:32, 1024 + 32 * u:1024 + 32 * (u + 1)], LKb[:, 64 * u + 32:64 * u + 64],
                            Rb[:, 32 * u:32 * (u + 1)], start=True, stop=True), inc=sT)
                    cur["t_att"] = sT.n
                    P.op("scalar", lambda e: e.activation(E0s[:, :], PA[0:32, 0:1024], AF.Exp, scale=SCALE),
                         waits=[(sT, cur["t_att"]), (sV, h1.get("v_d", 0)), (sT, h1.get("t_z", 0))], inc=sC)
                    P.op("scalar", lambda e: e.activation(E1s[:, :], PA[0:32, 1024:2048], AF.Exp, scale=SCALE),
                         inc=sC)
                    cur["c_exp"] = sC.n

                def mid(ht):
                    par = ht % 2
                    Rb = Rb2[:, 1024 * par:1024 * (par + 1)]
                    Dv = Rb[0:32, 0:512].rearrange("p (u a) -> p u a", u=32)
                    h1 = hist.get(ht - 1, {})
                    cur = hist[ht]
                    P.op("gpsimd", lambda e, ht=ht: e.dma_start(
                        V8b2[:, :].rearrange("p (g d) -> p g d", g=8), Vdv[ht]),
                        waits=[(sG, g_vdram), (sT, h1.get("t_out", 0))], inc=sG2, inck=16)
                    cur["g_v8"] = sG2.n
                    for hseg in range(2):
                        for srce, st_, sp_ in ((E0s, True, False), (E1s, False, True)):
                            P.op("tensor", lambda e, srce=srce, st_=st_, sp_=sp_, hseg=hseg: e.matmul(
                                PB[0:2, 512 * hseg:512 * (hseg + 1)], b2a[:, :],
                                srce[:, 512 * hseg:512 * (hseg + 1)],
                                start=st_, stop=sp_),
                                waits=[(sC, cur["c_exp"]), (sV, h1.get("v_sq", 0))] if st_ and hseg == 0 else (),
                                inc=sT)
                    cur["t_z"] = sT.n
                    P.op("scalar", lambda e: e.activation(Ztl[:, :], PB[0:2, 0:1024], AF.Ln),
                         waits=[(sT, cur["t_z"])], inc=sC)
                    cur["c_zl"] = sC.n
                    P.op("scalar", lambda e: e.activation(Zrb[:, :], Ztl[:, :], AF.Exp, scale=-1.0),
                         waits=[(sT, h1.get("t_zb", 0))], inc=sC)
                    cur["c_zr"] = sC.n
                    for lhs, coloff in ((sel2b, 0), (sel2l, 1024)):
                        for hseg in range(2):
                            P.op("tensor", lambda e, lhs=lhs, coloff=coloff, hseg=hseg: e.matmul(
                                PA[0:32, coloff + 512 * hseg:coloff + 512 * (hseg + 1)], lhs[:, :],
                                Zrb[0:2, 512 * hseg:512 * (hseg + 1)], start=True, stop=True),
                                waits=[(sC, cur["c_zr"]), (sV, v_sel2l)] if (coloff == 0 and hseg == 0) else (),
                                inc=sT)
                    cur["t_zb"] = sT.n
                    P.op("vector", lambda e: e.tensor_copy(Zb0s[:, :], PA[0:32, 0:1024]),
                         waits=[(sT, cur["t_zb"])], inc=sV)
                    P.op("vector", lambda e: e.tensor_copy(Zb1s[:, :], PA[0:32, 1024:2048]), inc=sV)
                    cur["v_zb"] = sV.n
                    P.op("vector", lambda e: e.tensor_tensor(E0s[:, :], E0s[:, :], Zb0s[:, :], ALU.mult), inc=sV)
                    P.op("vector", lambda e: e.tensor_tensor(E1s[:, :], E1s[:, :], Zb1s[:, :], ALU.mult), inc=sV)
                    P.op("vector", lambda e, Dv=Dv: e.tensor_tensor(Dv, E0v[:, :, 0:16], E1v[:, :, 16:32], ALU.subtract),
                         inc=sV)
                    cur["v_d"] = sV.n
                    for tau2 in range(2):
                        P.op("gpsimd", lambda e, tau2=tau2, ht=ht, Rb=Rb: e.dma_start(
                            Ddw[ht, tau2],
                            Rb[16 * tau2:16 * tau2 + 16, 0:512].rearrange("p (u a) -> p u a", u=32)),
                            waits=[(sV, cur["v_d"])], inc=sG3, inck=16)
                    cur["g_d"] = sG3.n
                    rsem = sS if ht % 2 == 0 else sS2
                    for tau in range(8):
                        P.op("sync", lambda e, tau=tau, ht=ht: e.dma_start(
                            RBD2[16 * tau:16 * tau + 16, :].rearrange("p (g c) -> p g c", g=8)[:, :, 16 * tau:16 * tau + 16],
                            Ddr[ht, tau]),
                            waits=[(sG3, cur["g_d"]), (sT, h1.get("t_out", 0)), (sGc, g_z)], inc=rsem, inck=16)
                    cur["s_rbd"] = (rsem, rsem.n)

                def back(ht):
                    if ht < 0:
                        return
                    cur = hist[ht]
                    h1 = hist.get(ht - 1, {})
                    owaits = [cur["s_rbd"], (sG2, cur["g_v8"]), (sC, cur["c_zl"])]
                    for g in range(8):
                        P.op("tensor", lambda e, g=g: e.matmul(
                            PB[0:128, 128 * g:128 * (g + 1)], V8b2[:, 128 * g:128 * (g + 1)],
                            RBD2[:, 128 * g:128 * (g + 1)], start=True, stop=True),
                            waits=owaits if g == 0 else (), inc=sT)
                    cur["t_out"] = sT.n
                    P.op("vector", lambda e: e.tensor_copy(OUTsb[:, :], PB[0:128, 0:1024]),
                         waits=[(sT, cur["t_out"]), (sC, cur["c_zl"])], inc=sV)
                    P.op("vector", lambda e: e.tensor_tensor(sqb2[:, :], OUTsb[:, :], OUTsb[:, :], ALU.mult),
                         inc=sV)
                    cur["v_sq"] = sV.n
                    cur["c_sq"] = sC.n
                    for hseg in range(2):
                        P.op("tensor", lambda e, hseg=hseg: e.matmul(
                            PB[0:1, 1024 + 512 * hseg:1024 + 512 * (hseg + 1)], o128b[:, :],
                            sqb2[:, 512 * hseg:512 * (hseg + 1)], start=True, stop=True),
                            waits=[(sV, cur["v_sq"]), (sV, h1.get("v_apply", 0)),
                                   (sC, h1.get("c_rs", 0))] if hseg == 0 else (), inc=sT)
                    cur["t_ms"] = sT.n
                    P.op("vector", lambda e: e.tensor_single_scalar(msb[:, :], PB[0:1, 1024:2048], 128.0 * EPS, ALU.add),
                         waits=[(sT, cur["t_ms"])], inc=sV)
                    cur["v_msb"] = sV.n
                    P.op("scalar", lambda e: e.activation(lnmsb[:, :], msb[:, :], AF.Ln),
                         waits=[(sV, cur["v_msb"])], inc=sC)
                    cur["c_ml"] = sC.n
                    P.op("scalar", lambda e: e.activation(rstdb[:, :], lnmsb[:, :], AF.Exp, scale=-0.5),
                         waits=[(sT, h1.get("t_rb", 0))], inc=sC)
                    cur["c_rs"] = sC.n
                    for hseg in range(2):
                        P.op("tensor", lambda e, hseg=hseg: e.matmul(
                            PB[0:128, 1024 + 512 * hseg:1024 + 512 * (hseg + 1)], o128br[:, :],
                            rstdb[0:1, 512 * hseg:512 * (hseg + 1)], start=True, stop=True),
                            waits=[(sC, cur["c_rs"])] if hseg == 0 else (), inc=sT)
                    cur["t_rb"] = sT.n
                    P.op("vector", lambda e, ht=ht: e.tensor_tensor(
                        ABv[:, ht], OUTv[:, :, :, :], RBv[:, :, :, :], ALU.mult),
                        waits=[(sT, cur["t_rb"])], inc=sV)
                    cur["v_apply"] = sV.n

                for ht in range(16):
                    front(ht)
                    back(ht - 1)
                    mid(ht)
                back(15)
                if KDBG2:
                    fin = hist[15]
                    P.op("gpsimd", lambda e: e.dma_start(dbg_es2[:, 0:1024], E0s[:, :]),
                         waits=[(sV, fin["v_d"])], inc=sG, inck=16)
                    P.op("gpsimd", lambda e: e.dma_start(dbg_es2[:, 1024:2048], E1s[:, :]), inc=sG, inck=16)
                    P.op("gpsimd", lambda e: e.dma_start(dbg_rb2x, Rb2[:, :]), inc=sG, inck=16)
                    P.op("gpsimd", lambda e: e.dma_start(dbg_lk2x, LK2[:, :]), inc=sG, inck=16)
                    P.op("gpsimd", lambda e: e.dma_start(dbg_zb2x[:, 0:1024], Zb0s[:, :]), inc=sG, inck=16)
                    P.op("gpsimd", lambda e: e.dma_start(dbg_zb2x[:, 1024:2048], Zb1s[:, :]), inc=sG, inck=16)
                    P.op("gpsimd", lambda e: e.dma_start(dbg_kt2, KT[:, :]), inc=sG, inck=16)
                    P.op("gpsimd", lambda e: e.dma_start(dbg_qt2, QT[:, :]), inc=sG, inck=16)
                P.flush(nc, "att")

        emit_att()
        if KDBG2:
            P.op("gpsimd", lambda e: e.dma_start(dbg_vd2, Vd), inc=sG, inck=16)
            P.op("gpsimd", lambda e: e.dma_start(dbg_lam2, lam_s[:, :]), inc=sG, inck=16)
            P.op("gpsimd", lambda e: e.dma_start(dbg_sel2, sel2l[:, :]), inc=sG, inck=16)
            P.op("gpsimd", lambda e: e.dma_start(dbg_dd2, Dd), inc=sG, inck=16)
            for nm_, sb_ in (("dbg_rbd2", "RBD2"), ("dbg_v82", "V8b2"), ("dbg_osb2", "OUTsb"), ("dbg_rst2", "rstdb")):
                pass
            P.op("gpsimd", lambda e: e.dma_start(dbg_outn2, AB[:, :]), inc=sG, inck=16)
            gw2 = sG.n
            P.op("gpsimd", lambda e: e.nop(), waits=[(sG, gw2)])
            P.flush(nc, "dbg2")
        if KDBG:
            dbg_dump("outn", dbg_outn, lambda: AB[:, :])

        def emit_wo():
            wov = wot.rearrange("(c p) n -> p c n", p=128)
            with nc.psum_tensor("pW0", [128, TPC], F32) as pW0, \
                 nc.psum_tensor("pW1", [128, TPC], F32) as pW1:
                pW = (pW0, pW1)
                tdone = {}
                vres = {}
                for co in range(NCH):
                    par = co % 2
                    slot = KT[:, DIM * par:DIM * (par + 1)]
                    waits = []
                    if co >= 2:
                        waits.append((sT, tdone[co - 2]))
                    wsem = sW if par == 0 else sW2
                    P.op("sync", lambda e, slot=slot, co=co: e.dma_start(
                        slot.rearrange("p (c n) -> p c n", c=NCH),
                        wov[:, :, 128 * co:128 * (co + 1)]), waits=waits, inc=wsem, inck=16)
                    w_here = wsem.n
                    mwaits = [(wsem, w_here)]
                    if co >= 2:
                        mwaits.append((sV, vres[co - 2]))
                    nm = 0
                    for a in range(NCH):
                        for hseg in range(2):
                            P.op("tensor", lambda e, par=par, slot=slot, a=a, hseg=hseg: e.matmul(
                                pW[par][:, 512 * hseg:512 * (hseg + 1)],
                                slot[:, 128 * a:128 * (a + 1)],
                                AB[:, TPC * a + 512 * hseg:TPC * a + 512 * (hseg + 1)],
                                start=(a == 0), stop=False),
                                waits=mwaits if nm == 0 else (), inc=sT)
                            nm += 1
                    for hseg in range(2):
                        P.op("tensor", lambda e, par=par, co=co, hseg=hseg: e.matmul(
                            pW[par][:, 512 * hseg:512 * (hseg + 1)],
                            bo_sb[0:1, 128 * co:128 * (co + 1)],
                            oNb[0:1, 512 * hseg:512 * (hseg + 1)],
                            start=False, stop=True), inc=sT)
                    tdone[co] = sT.n
                    P.op("vector", lambda e, co=co, par=par: e.tensor_tensor(
                        x_sb[:, TPC * co:TPC * (co + 1)], x_sb[:, TPC * co:TPC * (co + 1)],
                        pW[par][:, :], ALU.add), waits=[(sT, tdone[co])], inc=sV)
                    vres[co] = sV.n
                P.flush(nc, "wo")

        emit_wo()
        if KDBG:
            dbg_dump("x2", dbg_x2, lambda: x_sb[:, :])
        emit_ln2("2")
        if KDBG:
            dbg_dump("ln2", dbg_ln2, lambda: AB[:, :])

        def emit_ffn():
            wgv = wgp.rearrange("(c p) n -> p c n", p=128)
            wuv = wup.rearrange("(c p) n -> p c n", p=128)
            wdv = wdp.rearrange("(m b j p) n -> b p m j n", m=4, b=4, j=4, p=128)
            with nc.psum_tensor("PC1", [128, 2048], F32) as PC1, \
                 nc.psum_tensor("PC2", [128, 2048], F32) as PC2, \
                 nc.sbuf_tensor("Gs", [128, TPC], F32) as Gs, \
                 nc.sbuf_tensor("q4", [128, 4 * TPC], BF16) as q4, \
                 nc.sbuf_tensor("wins", [128, 4 * TPC], BF16) as wins, \
                 nc.sbuf_tensor("tbuf", [128, TPC], BF16) as tbuf:
                PC = (PC1, PC2)
                wg_slot = lambda par: KT[:, 2048 * par:2048 * (par + 1)]
                wu_slot = lambda par: KT[:, 4096 + 2048 * par:4096 + 2048 * (par + 1)]
                wd_slot = lambda par: KT[:, 8192 + 2048 * par:8192 + 2048 * (par + 1)]
                t_gu = {}
                c_gu = {}
                v_h = {}
                g_sq = {}
                v_mask = 0
                t_down_prev = 0
                v_yadd = {}
                prev_q_applied = 0
                cidx = 0
                for q in range(4):
                    v_mask_j = {}
                    for j in range(4):
                        for m in range(4):
                            gidx = 4 * q + j + 16 * m
                            s = 4 * m + j
                            par = cidx % 2
                            waits = []
                            if cidx >= 2:
                                waits.append((sT, t_gu.get(cidx - 2, 0)))
                            wsem = sW if par == 0 else sW2
                            P.op("sync", lambda e, par=par, gidx=gidx: e.dma_start(
                                wg_slot(par).rearrange("p (c n) -> p c n", c=NCH),
                                wgv[:, :, 128 * gidx:128 * (gidx + 1)]), waits=waits, inc=wsem, inck=16)
                            P.op("sync", lambda e, par=par, gidx=gidx: e.dma_start(
                                wu_slot(par).rearrange("p (c n) -> p c n", c=NCH),
                                wuv[:, :, 128 * gidx:128 * (gidx + 1)]), inc=wsem, inck=16)
                            mwaits = [(wsem, wsem.n)]
                            if cidx >= 2:
                                mwaits.append((sC, c_gu.get(cidx - 2, 0)))
                                mwaits.append((sV, v_h.get(cidx - 2, 0)))
                            if q > 0 and cidx % 16 < 2:
                                mwaits.append((sV, v_yadd.get((q - 1, 15 - (1 - cidx % 16)), 0)))
                            nm = 0
                            for ci in range(NCH):
                                for hseg in range(2):
                                    P.op("tensor", lambda e, par=par, ci=ci, hseg=hseg: e.matmul(
                                        PC[par][:, 512 * hseg:512 * (hseg + 1)],
                                        wg_slot(par)[:, 128 * ci:128 * (ci + 1)],
                                        AB[:, TPC * ci + 512 * hseg:TPC * ci + 512 * (hseg + 1)],
                                        start=(ci == 0), stop=(ci == NCH - 1)),
                                        waits=mwaits if nm == 0 else (), inc=sT)
                                    nm += 1
                            for ci in range(NCH):
                                for hseg in range(2):
                                    P.op("tensor", lambda e, par=par, ci=ci, hseg=hseg: e.matmul(
                                        PC[par][:, 1024 + 512 * hseg:1024 + 512 * (hseg + 1)],
                                        wu_slot(par)[:, 128 * ci:128 * (ci + 1)],
                                        AB[:, TPC * ci + 512 * hseg:TPC * ci + 512 * (hseg + 1)],
                                        start=(ci == 0), stop=(ci == NCH - 1)), inc=sT)
                            t_gu[cidx] = sT.n
                            P.op("scalar", lambda e, par=par: e.activation(
                                Gs[:, :], PC[par][:, 0:1024], AF.Silu),
                                waits=[(sT, t_gu[cidx]), (sV, v_h.get(cidx - 1, 0))], inc=sC)
                            c_gu[cidx] = sC.n
                            hwaits = [(sC, c_gu[cidx])]
                            if q > 0:
                                hwaits.append((sT, t_down_prev))
                            P.op("vector", lambda e, par=par, s=s: e.tensor_tensor(
                                QT[:, TPC * s:TPC * (s + 1)], Gs[:, :], PC[par][:, 1024:2048], ALU.mult),
                                waits=hwaits, inc=sV)
                            v_h[cidx] = sV.n
                            P.op("vector", lambda e, m=m, s=s: e.tensor_tensor(
                                q4[:, TPC * m:TPC * (m + 1)], QT[:, TPC * s:TPC * (s + 1)],
                                QT[:, TPC * s:TPC * (s + 1)], ALU.mult),
                                waits=[(sV, v_h[cidx])], inc=sV)
                            g_sq[(q, j, m)] = sV.n
                            cidx += 1
                        g_here = sV.n
                        qs = [q4[:, TPC * m:TPC * (m + 1)] for m in range(4)]
                        ws = [wins[:, TPC * m:TPC * (m + 1)] for m in range(4)]
                        mw = []
                        def vop(build, waits=()):
                            P.op("vector", build, waits=waits, inc=sV)
                        vop(lambda e: e.tensor_tensor(ws[1], qs[0], qs[1], ALU.is_ge), waits=mw)
                        vop(lambda e: e.tensor_copy(ws[0], ws[1]))
                        vop(lambda e: e.tensor_tensor(ws[2], qs[0], qs[2], ALU.is_ge))
                        vop(lambda e: e.tensor_tensor(ws[0], ws[0], ws[2], ALU.add))
                        vop(lambda e: e.tensor_tensor(ws[3], qs[0], qs[3], ALU.is_ge))
                        vop(lambda e: e.tensor_tensor(ws[0], ws[0], ws[3], ALU.add))
                        vop(lambda e: e.tensor_tensor(tbuf[:, :], qs[1], qs[2], ALU.is_ge))
                        vop(lambda e: e.tensor_tensor(ws[1], ws[1], tbuf[:, :], ALU.subtract))
                        vop(lambda e: e.tensor_tensor(ws[2], ws[2], tbuf[:, :], ALU.add))
                        vop(lambda e: e.tensor_tensor(tbuf[:, :], qs[1], qs[3], ALU.is_ge))
                        vop(lambda e: e.tensor_tensor(ws[1], ws[1], tbuf[:, :], ALU.subtract))
                        vop(lambda e: e.tensor_tensor(ws[3], ws[3], tbuf[:, :], ALU.add))
                        vop(lambda e: e.tensor_tensor(tbuf[:, :], qs[2], qs[3], ALU.is_ge))
                        vop(lambda e: e.tensor_tensor(ws[2], ws[2], tbuf[:, :], ALU.subtract))
                        vop(lambda e: e.tensor_tensor(ws[3], ws[3], tbuf[:, :], ALU.add))
                        vop(lambda e: e.tensor_single_scalar(ws[0], ws[0], 1.5, ALU.is_ge))
                        vop(lambda e: e.tensor_single_scalar(ws[1], ws[1], -0.6, ALU.is_le))
                        vop(lambda e: e.tensor_single_scalar(ws[2], ws[2], 0.4, ALU.is_le))
                        vop(lambda e: e.tensor_single_scalar(ws[3], ws[3], 1.4, ALU.is_le))
                        for m in range(4):
                            s = 4 * m + j
                            vop(lambda e, m=m, s=s: e.tensor_tensor(
                                QT[:, TPC * s:TPC * (s + 1)], QT[:, TPC * s:TPC * (s + 1)],
                                wins[:, TPC * m:TPC * (m + 1)], ALU.mult))
                        v_mask = sV.n
                        v_mask_j[j] = sV.n
                    tdd = {}
                    for co in range(NCH):
                        par = co % 2
                        waits = [(sT, t_gu[cidx - 1])]
                        if co >= 2:
                            waits = [(sT, tdd[co - 2])]
                        wsem = sWd if par == 0 else sWd2
                        for mm_ in range(4):
                            P.op("sync", lambda e, par=par, q=q, co=co, mm_=mm_: e.dma_start(
                                wd_slot(par).rearrange("p (m j n) -> p m j n", m=4, j=4)[:, mm_],
                                wdv[q][:, mm_, :, 128 * co:128 * (co + 1)]),
                                waits=waits if mm_ == 0 else (), inc=wsem, inck=16)
                        mwaits = [(wsem, wsem.n), (sV, v_mask_j[3])]
                        if co >= 2:
                            mwaits.append((sV, v_yadd[(q, co - 2)]))
                        nm = 0
                        for s in range(16):
                            for hseg in range(2):
                                P.op("tensor", lambda e, par=par, s=s, hseg=hseg: e.matmul(
                                    PC[par][:, 512 * hseg:512 * (hseg + 1)],
                                    wd_slot(par)[:, 128 * s:128 * (s + 1)],
                                    QT[:, TPC * s + 512 * hseg:TPC * s + 512 * (hseg + 1)],
                                    start=(s == 0), stop=(s == 15)),
                                    waits=mwaits if nm == 0 else (), inc=sT)
                                nm += 1
                        tdd[co] = sT.n
                        P.op("vector", lambda e, co=co, par=par: e.tensor_tensor(
                            x_sb[:, TPC * co:TPC * (co + 1)], x_sb[:, TPC * co:TPC * (co + 1)],
                            PC[par][:, 0:1024], ALU.add), waits=[(sT, tdd[co])], inc=sV)
                        v_yadd[(q, co)] = sV.n
                        if q == 3:
                            P.op("gpsimd", lambda e, co=co: e.dma_start(
                                y_d[128 * co:128 * (co + 1), :], x_sb[:, TPC * co:TPC * (co + 1)]),
                                waits=[(sV, v_yadd[(q, co)])], inc=sG, inck=16)
                    t_down_prev = sT.n
                g_final = sG.n
                P.op("gpsimd", lambda e: e.nop(), waits=[(sG, g_final)])
                P.flush(nc, "ffn")

        emit_ffn()

    stk.close()
    return nc


def _prep_inputs(inputs):
    permQ = np.empty(DIM, np.int64)
    for s in range(2):
        for h in range(H):
            for d in range(DH):
                permQ[(s * H + h) * DH + d] = h * 128 + 2 * d + s
    permI = np.empty(IFFN, np.int64)
    for g in range(2048):
        for m in range(4):
            permI[m * 2048 + g] = 4 * g + m
    b16 = lambda a: np.ascontiguousarray(a).astype(BD)
    f32 = lambda a: np.ascontiguousarray(a, dtype=np.float32)
    common = {
        "wqt": b16(inputs["L0_attn_Wq"][permQ, :].T),
        "wkt": b16(inputs["L0_attn_Wk"][permQ, :].T),
        "wvt": b16(inputs["L0_attn_Wv"].T),
        "wot": b16(inputs["L0_attn_Wo"].T),
        "wgp": b16(inputs["L0_ffn_w_gate"][:, permI]),
        "wup": b16(inputs["L0_ffn_w_up"][:, permI]),
        "wdp": b16(inputs["L0_ffn_w_down"][permI, :]),
        "bo": b16(inputs["L0_attn_bo"].reshape(1, DIM)),
        "lqk": f32(np.concatenate([inputs["L0_attn_lq1"], inputs["L0_attn_lq2"],
                                   inputs["L0_attn_lk1"], inputs["L0_attn_lk2"]]).reshape(1, 256)),
    }
    b2a = np.zeros((32, 2), np.float32)
    for k in range(32):
        b2a[k, k // 16] = 1.0
    sel2 = np.zeros((2, 32), np.float32)
    for r in range(32):
        sel2[r // 16, r] = 1.0
    common["b2a"] = b16(b2a)
    common["sel2b"] = b16(sel2)
    common["sel2f"] = f32(sel2)
    common["o128b"] = b16(np.ones((128, 1), np.float32))
    common["o128f"] = f32(np.ones((1, 128), np.float32))
    common["o128br"] = b16(np.full((1, 128), (1.0 - LAM_INIT) * float(np.sqrt(128.0)), np.float32))
    common["o2f"] = f32(np.ones((1, 2), np.float32))
    common["oNb"] = b16(np.ones((1, TPC), np.float32))
    common["cvf"] = f32(np.concatenate([np.array(CONSTS_F32, np.float64),
                                        np.zeros(128 - len(CONSTS_F32))]).reshape(1, -1))
    common["cvb"] = np.concatenate([np.array(CONSTS_B16, np.float32),
                                    np.zeros(256 - len(CONSTS_B16), np.float32)]).astype(BD).reshape(1, -1)
    x = np.asarray(inputs["L0_attn_x"], dtype=np.float32).reshape(8192, DIM)
    in_maps = []
    for c in range(NCORES):
        m = dict(common)
        m["xT"] = np.ascontiguousarray(x[TPC * c:TPC * (c + 1), :].T)
        in_maps.append(m)
    return in_maps


def kernel(**inputs):
    in_maps = _prep_inputs(inputs)
    nc = build_program()
    res = run_bass_kernel_spmd(nc, in_maps, core_ids=list(range(NCORES)))
    out = np.empty((8192, DIM), np.float32)
    for c in range(NCORES):
        out[TPC * c:TPC * (c + 1), :] = res.results[c]["y"].T
    return out.reshape(1, 8192, DIM)


